# version 04
# speedup vs baseline: 1.2964x; 1.0026x over previous
import numpy as np
from contextlib import ExitStack
import concourse.bass as bass
import concourse.mybir as mybir
from concourse.bass_utils import run_bass_kernel_spmd

F32 = mybir.dt.float32
BF16 = mybir.dt.bfloat16
I32 = mybir.dt.int32
AF = mybir.ActivationFunctionType
ALU = mybir.AluOpType
AX = mybir.AxisListType

NCORE = 8
TOK = 2048
NT = TOK // 128
D = 1024
TWO_PI = float(2 * np.pi)
INV_2PI = float(1.0 / (2 * np.pi))


class PB:
    ENGS = ("sync", "gpsimd", "tensor", "vector", "scalar")
    PARANOID = False

    DMA_SEMS = ("LD", "GL", "WL", "ST")

    def __init__(self):
        self.nc = bass.Bass("TRN2", target_bir_lowering=False, debug=False)
        self.es = ExitStack()
        self.streams = {k: [] for k in self.ENGS}
        self._sems = {}
        self._dma_alt = {}

    def sb(self, name, shape, dt=F32):
        return self.es.enter_context(self.nc.sbuf_tensor(name, shape, dt))

    def psum(self, name, shape, dt=F32):
        return self.es.enter_context(self.nc.psum_tensor(name, shape, dt))

    def din(self, name, shape, dt=F32):
        return self.nc.dram_tensor(name, shape, dt, kind="ExternalInput").ap()

    def dout(self, name, shape, dt=F32):
        return self.nc.dram_tensor(name, shape, dt, kind="ExternalOutput").ap()

    def sem(self, name):
        if name not in self._sems:
            self._sems[name] = [self.es.enter_context(self.nc.semaphore(name)), 0]
        return name

    def do(self, eng, build, deps=(), inc=None, k=1):
        if eng == "gpsimd" and inc == "LD":
            inc = "GL"
        if inc in self.DMA_SEMS:
            idx = self._dma_alt.get(inc, 0)
            self._dma_alt[inc] = idx + 1
            real = inc + str(idx % 2)
            self.sem(real)
            deps = list(deps) + [(real, self._sems[real][1])]
            inc = real
        tok = None
        if self.PARANOID:
            deps = list(deps) + [(s, self._sems[s][1]) for s in self._sems
                                 if self._sems[s][1] > 0]
        if inc is not None:
            self.sem(inc)
            self._sems[inc][1] += k
            tok = (inc, self._sems[inc][1])
        semmap = self._sems
        deps = tuple(d for d in deps if d is not None)

        def thunk(e, build=build, deps=deps, inc=inc, k=k):
            for (sn, sv) in deps:
                e.wait_ge(semmap[sn][0], sv)
            ins = build(e)
            if inc is not None:
                ins.then_inc(semmap[inc][0], k)
        self.streams[eng].append(thunk)
        return tok

    def dma_done(self, name):
        out = []
        for sfx in ("0", "1"):
            s = name + sfx
            if s in self._sems and self._sems[s][1] > 0:
                out.append((s, self._sems[s][1]))
        return out

    def wait(self, eng, *deps):
        semmap = self._sems
        deps = tuple(d for d in deps if d is not None)

        def thunk(e, deps=deps):
            for (sn, sv) in deps:
                e.wait_ge(semmap[sn][0], sv)
        self.streams[eng].append(thunk)

    def finalize(self):
        with self.es:
            with self.nc.Block() as blk:
                for name in self.ENGS:
                    ops = self.streams[name]
                    getattr(blk, name)(lambda e, ops=ops: [op(e) for op in ops])
        return self.nc


def tok_view(dram_ap):
    return dram_ap.rearrange("(t p) d -> p t d", p=128)



def build_l0_attn(debug=False):
    P = PB()
    x = P.din("x", [TOK, D])
    w1t = P.din("w1t", [16, 4096])
    b1c = P.din("b1c", [128, 32])
    w2t = P.din("w2t", [4096, D])
    rows = P.din("rows", [5, D])
    ident = P.din("ident", [128, 128])
    xo = P.dout("xo", [TOK, D])
    dbg = P.dout("dbg", [8, 128, D]) if debug else None
    dbgw = P.dout("dbgw", [16, 4096]) if debug else None
    dbgb = P.dout("dbgb", [128, 32]) if debug else None

    xtl = [P.sb("xtl0", [128, D]), P.sb("xtl1", [128, D])]
    w1s = P.sb("w1s", [16, 4096], BF16)
    w2s = P.sb("w2s", [128, 32 * D], BF16)
    b1s = P.sb("b1s", [128, 32])
    idb = P.sb("idb", [128, 128], BF16)
    rows_s = P.sb("rows_s", [1, 5 * D])
    bc = P.sb("bc", [128, 5 * D])
    ones_row = P.sb("ones_row", [1, 128])
    epsb = P.sb("epsb", [128, 1])
    sA = P.sb("sA", [128, D])
    sB = P.sb("sB", [128, D])
    sC = P.sb("sC", [128, D])
    sD = P.sb("sD", [128, D])
    sY = P.sb("sY", [128, D])
    sI = P.sb("sI", [128, D], I32)
    sScr = P.sb("sScr", [128, D])
    st1 = P.sb("st1", [128, 8])
    s16a = P.sb("s16a", [128, 16])
    s16b = P.sb("s16b", [128, 16])
    s16c = P.sb("s16c", [128, 16])
    s16i = P.sb("s16i", [128, 16], I32)
    q16b = P.sb("q16b", [128, 16], BF16)
    qfm = P.sb("qfm", [16, 128], BF16)
    reluh = P.sb("reluh", [128, 32 * 128], BF16)
    outt = [P.sb("outt0", [128, D]), P.sb("outt1", [128, D])]

    ptq = P.psum("ptq", [128, 128], BF16)
    ph = [P.psum("ph0", [128, 128]), P.psum("ph1", [128, 128])]
    po = P.psum("po", [128, D])
    pb2 = P.psum("pb2", [128, 512])

    LD, DV, AC, TE, ST = "LD", "DV", "AC", "TE", "ST"
    for s in (LD, DV, AC, TE, ST):
        P.sem(s)

    xv = x.rearrange("(t p) d -> p t d", p=128)
    t_b1 = P.do("sync", lambda e: e.dma_start(out=b1s[:], in_=b1c[:]), inc=LD, k=16)
    t_rows = P.do("sync", lambda e: e.dma_start(out=rows_s[:], in_=rows.rearrange("r d -> (r d)")[None, :]), inc=LD, k=16)
    t_w1 = P.do("gpsimd", lambda e: e.dma_start(out=w1s[:], in_=w1t[:]), inc=LD, k=16)
    t_id = P.do("gpsimd", lambda e: e.dma_start(out=idb[:], in_=ident[:]), inc=LD, k=16)
    t_w2 = P.do("gpsimd", lambda e: e.dma_start(out=w2s[:], in_=w2t.rearrange("(c p) n -> p c n", p=128)), inc=LD, k=16)

    if debug:
        P.do("gpsimd", lambda e: e.dma_start(out=dbgw[:], in_=w1s[:]), deps=[t_w1], inc=ST, k=16)
        P.do("gpsimd", lambda e: e.dma_start(out=dbgb[:], in_=b1s[:]), deps=[t_b1], inc=ST, k=16)
    t_ones = P.do("vector", lambda e: e.memset(ones_row[:], 1.0), inc=DV)
    t_eps = P.do("vector", lambda e: e.memset(epsb[:], 1e-5), inc=DV)

    ev_toks = []
    for i in range(5):
        for hhalf in range(2):
            tm = P.do("tensor",
                      lambda e, i=i, h=hhalf: e.matmul(
                          pb2[:, :], ones_row[0:1, :],
                          rows_s[0:1, i * D + h * 512:i * D + (h + 1) * 512],
                          start=True, stop=True),
                      deps=[t_rows, t_ones] + ev_toks[-1:], inc=TE)
            ev = P.do("vector",
                      lambda e, i=i, h=hhalf: e.tensor_copy(
                          bc[:, i * D + h * 512:i * D + (h + 1) * 512], pb2[:, :]),
                      deps=[tm], inc=DV)
            ev_toks.append(ev)
    t_bc = ev_toks[-1]

    st_toks = [None, None]
    xl_free = [None, None]
    xl_tok = [None, None]
    po_free = None
    ph_free = [None, None]

    def emit_ln_stats(src_ap, deps):
        m1 = P.do("vector", lambda e: e.reduce_sum(st1[:, 0:1], src_ap, AX.X), deps=deps, inc=DV)
        q1 = P.do("scalar", lambda e: e.activation(sScr[:], src_ap, AF.Square, accum_out=st1[:, 1:2]),
                  deps=deps, inc=AC)
        m1 = P.do("vector", lambda e: e.tensor_scalar_mul(st1[:, 0:1], st1[:, 0:1], 1.0 / D), deps=[m1], inc=DV)
        m2 = P.do("vector", lambda e: e.tensor_mul(st1[:, 2:3], st1[:, 0:1], st1[:, 0:1]), deps=[m1], inc=DV)
        vv = P.do("vector", lambda e: e.scalar_tensor_tensor(st1[:, 3:4], st1[:, 1:2], 1.0 / D, st1[:, 2:3],
                                                             ALU.mult, ALU.subtract), deps=[m2, q1], inc=DV)
        lv = P.do("scalar", lambda e: e.activation(st1[:, 3:4], st1[:, 3:4], AF.Ln, bias=epsb[:, 0:1]),
                  deps=[vv, t_eps], inc=AC)
        return P.do("scalar", lambda e: e.activation(st1[:, 4:5], st1[:, 3:4], AF.Exp, scale=-0.5),
                    deps=[lv], inc=AC)

    for t in range(NT):
        par = t % 2
        xl_tok[par] = P.do("sync", lambda e, t=t, par=par: e.dma_start(out=xtl[par][:], in_=xv[:, t, :]),
                           deps=[xl_free[par]], inc=LD, k=16)
        xt = xtl[par][:]
        rs0 = emit_ln_stats(xt, [xl_tok[par]])
        ty = P.do("vector", lambda e, xt=xt: e.tensor_scalar(
            sY[:], xt, st1[:, 0:1], st1[:, 4:5], ALU.subtract, ALU.mult),
            deps=[rs0], inc=DV)
        a = P.do("vector", lambda e: e.tensor_scalar_mul(sA[:], sY[:], INV_2PI), deps=[ty], inc=DV)
        a = P.do("vector", lambda e: e.tensor_copy(sI[:], sA[:]), deps=[a], inc=DV)
        a = P.do("vector", lambda e: e.tensor_copy(sB[:], sI[:]), deps=[a], inc=DV)
        a = P.do("vector", lambda e: e.scalar_tensor_tensor(sA[:], sB[:], -TWO_PI, sY[:], ALU.mult, ALU.add),
                 deps=[a], inc=DV)
        s_ = P.do("scalar", lambda e: e.activation(sB[:], sA[:], AF.Sin, scale=0.5), deps=[a], inc=AC)
        s_ = P.do("scalar", lambda e: e.activation(sC[:], sB[:], AF.Square), deps=[s_], inc=AC)
        a = P.do("vector", lambda e: e.tensor_scalar(sB[:], sC[:], -2.0, 1.0, ALU.mult, ALU.add),
                 deps=[s_], inc=DV)
        for h in range(16):
            a = P.do("vector", lambda e, h=h: e.tensor_tensor_scan(
                sC[:, h * 64:(h + 1) * 64], sB[:, h * 64:(h + 1) * 64],
                sB[:, h * 64:(h + 1) * 64], 1.0, ALU.mult, ALU.bypass),
                deps=[a], inc=DV)
        if debug and t == 0:
            P.do("gpsimd", lambda e: e.dma_start(out=dbg[0], in_=sY[:]), deps=[ty], inc=ST, k=16)
            P.do("gpsimd", lambda e: e.dma_start(out=dbg[1], in_=sB[:]), deps=[a], inc=ST, k=16)
            P.do("gpsimd", lambda e: e.dma_start(out=dbg[2], in_=sC[:]), deps=[a], inc=ST, k=16)
        tz = P.do("vector", lambda e: e.tensor_add(sA[:], sY[:], sC[:]), deps=[a], inc=DV)
        rs = emit_ln_stats(sA[:], [tz])
        a = P.do("vector", lambda e: e.tensor_scalar(sC[:], sA[:], st1[:, 0:1], st1[:, 4:5],
                                                     ALU.subtract, ALU.mult), deps=[rs], inc=DV)
        a = P.do("vector", lambda e: e.tensor_mul(sC[:], sC[:], bc[:, 1 * D:2 * D]), deps=[a, t_bc], inc=DV)
        tx1 = P.do("vector", lambda e: e.tensor_add(sC[:], sC[:], bc[:, 2 * D:3 * D]), deps=[a], inc=DV)
        if debug and t == 0:
            P.do("gpsimd", lambda e: e.dma_start(out=dbg[3], in_=sC[:]), deps=[tx1], inc=ST, k=16)
        a = P.do("vector", lambda e: e.tensor_scalar_mul(s16a[:], sC[:, 0:16], INV_2PI), deps=[tx1], inc=DV)
        a = P.do("vector", lambda e: e.tensor_copy(s16i[:], s16a[:]), deps=[a], inc=DV)
        a = P.do("vector", lambda e: e.tensor_copy(s16b[:], s16i[:]), deps=[a], inc=DV)
        a = P.do("vector", lambda e: e.scalar_tensor_tensor(s16a[:], s16b[:], -TWO_PI, sC[:, 0:16],
                                                            ALU.mult, ALU.add), deps=[a], inc=DV)
        s_ = P.do("scalar", lambda e: e.activation(s16b[:], s16a[:], AF.Sin, scale=0.5), deps=[a], inc=AC)
        s_ = P.do("scalar", lambda e: e.activation(s16c[:], s16b[:], AF.Square), deps=[s_], inc=AC)
        a = P.do("vector", lambda e: e.tensor_scalar(s16b[:], s16c[:], -2.0, 1.0, ALU.mult, ALU.add),
                 deps=[s_], inc=DV)
        a = P.do("vector", lambda e: e.tensor_tensor_scan(s16c[:], s16b[:], s16b[:], 1.0, ALU.mult, ALU.bypass),
                 deps=[a], inc=DV)
        tq = P.do("vector", lambda e: e.tensor_copy(q16b[:], s16c[:]), deps=[a], inc=DV)
        tt = P.do("tensor", lambda e: e.transpose(ptq[0:16, 0:128], q16b[:], idb[:]),
                  deps=[tq, t_id], inc=TE)
        tqf = P.do("vector", lambda e: e.tensor_copy(qfm[:], ptq[0:16, 0:128]), deps=[tt], inc=DV)
        ac_ev = []
        for fc in range(32):
            pp = par if False else fc % 2
            tm = P.do("tensor", lambda e, fc=fc, pp=pp: e.matmul(
                ph[pp][:, :], w1s[0:16, fc * 128:(fc + 1) * 128], qfm[0:16, 0:128],
                start=True, stop=True),
                deps=[tqf, t_w1, ph_free[pp]], inc=TE)
            ev = P.do("vector", lambda e, fc=fc, pp=pp: e.tensor_scalar(
                reluh[:, fc * 128:(fc + 1) * 128], ph[pp][:, :], b1s[:, fc:fc + 1], 0.0,
                ALU.add, ALU.max),
                deps=[tm, t_b1], inc=DV)
            ph_free[pp] = ev
            ac_ev.append(ev)
        lastmm = None
        for fc in range(32):
            for hh in range(2):
                lastmm = P.do("tensor", lambda e, fc=fc, hh=hh: e.matmul(
                    po[:, hh * 512:(hh + 1) * 512],
                    reluh[:, fc * 128:(fc + 1) * 128],
                    w2s[:, fc * D + hh * 512:fc * D + (hh + 1) * 512],
                    start=(fc == 0), stop=(fc == 31)),
                    deps=([ac_ev[-1], t_w2, po_free] if fc == 0 and hh == 0 else []), inc=TE)
        a = P.do("vector", lambda e: e.tensor_add(sD[:], po[:, :], bc[:, 0:D]), deps=[lastmm], inc=DV)
        if debug and t == 0:
            P.do("gpsimd", lambda e: e.dma_start(out=dbg[5], in_=sD[:]), deps=[a], inc=ST, k=16)
            for j in range(4):
                P.do("gpsimd", lambda e, j=j: e.dma_start(out=dbg[4 + j if j < 3 else 7], in_=reluh[:, j*1024:(j+1)*1024]), deps=[a], inc=ST, k=16)
        po_free = a
        tz2 = P.do("vector", lambda e: e.tensor_add(sD[:], sD[:], sC[:]), deps=[a], inc=DV)
        rs = emit_ln_stats(sD[:], [tz2])
        a = P.do("vector", lambda e: e.tensor_scalar(sY[:], sD[:], st1[:, 0:1], st1[:, 4:5],
                                                     ALU.subtract, ALU.mult), deps=[rs], inc=DV)
        a = P.do("vector", lambda e: e.tensor_mul(sY[:], sY[:], bc[:, 3 * D:4 * D]), deps=[a], inc=DV)
        a = P.do("vector", lambda e: e.tensor_add(sY[:], sY[:], bc[:, 4 * D:5 * D]), deps=[a], inc=DV)
        if debug and t == 0:
            P.do("gpsimd", lambda e: e.dma_start(out=dbg[6], in_=sY[:]), deps=[a], inc=ST, k=16)
        tfin = P.do("vector", lambda e, xt=xt, par=par: e.tensor_add(outt[par][:], sY[:], xt),
                    deps=[a, st_toks[par]], inc=DV)
        xl_free[par] = tfin
        st_toks[par] = P.do("gpsimd", lambda e, t=t, par=par: e.dma_start(
            out=xo.rearrange("(t p) d -> p t d", p=128)[:, t, :], in_=outt[par][:]),
            deps=[tfin], inc=ST, k=16)

    P.wait("gpsimd", *P.dma_done("ST"))
    return P.finalize()


def prep_l0_attn(d):
    w1 = d["L0_attn_w1"]; b1 = d["L0_attn_b1"]; w2 = d["L0_attn_w2"]; b2 = d["L0_attn_b2"]
    com = {
        "w1t": np.ascontiguousarray(w1.T),
        "b1c": np.ascontiguousarray(b1.reshape(32, 128).T),
        "w2t": np.ascontiguousarray(w2.T),
        "rows": np.ascontiguousarray(np.stack([b2, d["L0_attn_gamma1"], d["L0_attn_beta1"],
                                               d["L0_attn_gamma2"], d["L0_attn_beta2"]])),
        "ident": np.eye(128, dtype=np.float32),
    }
    return com


_L0_ATTN_NC = None


def launch_l0_attn(xflat, d, debug=False):
    global _L0_ATTN_NC
    if _L0_ATTN_NC is None or debug:
        _L0_ATTN_NC = build_l0_attn(debug)
    com = prep_l0_attn(d)
    in_maps = []
    for c in range(NCORE):
        m = dict(com)
        m["x"] = np.ascontiguousarray(xflat[c * TOK:(c + 1) * TOK])
        in_maps.append(m)
    res = run_bass_kernel_spmd(_L0_ATTN_NC, in_maps, core_ids=list(range(NCORE)))
    out = np.concatenate([res.results[c]["xo"] for c in range(NCORE)], axis=0)
    if debug:
        return out, res.results[0].get("dbg")
    return out


def kernel(**inputs):
    assert inputs["L0_attn_attn_circuit_params"].shape == (16, 64)
    assert inputs["L0_attn_ffn_circuit_params"].shape == (16,)
    x = np.ascontiguousarray(inputs["L0_attn_x"].reshape(2 * 8192, 1024))
    x = launch_l0_attn(x, inputs)
    x = launch_l0_ffn(x, inputs)
    x = launch_l1_attn(x, inputs)
    x = launch_l1_ffn(x, inputs)
    x = launch_l2_attn(x, inputs)
    x = launch_l2_ffn(x, inputs)
    x = launch_l3_attn(x, inputs)
    x = launch_l3_ffn(x, inputs)
    return x.reshape(2, 8192, 1024)



def build_l0_ffn():
    P = PB()
    x = P.din("x", [TOK, D])
    w1t = P.din("w1t", [9 * D, D])
    w2t = P.din("w2t", [9 * D, D])
    gwt = P.din("gwt", [D, 8])
    lbr = P.din("lbr", [1, 8])
    ident = P.din("ident", [128, 128])
    xo = P.dout("xo", [TOK, D])

    HT = 8
    TOKC = HT * 128

    xtl = [P.sb("xtl0", [128, D]), P.sb("xtl1", [128, D])]
    idb = P.sb("idb", [128, 128], BF16)
    gws = P.sb("gws", [128, 8 * 8])
    idf = P.sb("idf", [128, 128])
    yf32 = P.sb("yf32", [128, D])
    lbrow = P.sb("lbrow", [1, 8])
    lbb = P.sb("lbb", [128, 8])
    ones_row = P.sb("ones_row", [1, 128])
    epsb = P.sb("epsb", [128, 1])
    st1 = P.sb("st1", [128, 8])
    sScr = P.sb("sScr", [128, D])
    sY = P.sb("sY", [128, D])
    sBF = P.sb("sBF", [128, D], BF16)
    yfm = P.sb("yfm", [128, 8 * TOKC], BF16)
    acc = P.sb("acc", [128, HT * D])
    w1s = [P.sb("w1s0", [128, 8 * D], BF16), P.sb("w1s1", [128, 8 * D], BF16)]
    w2s = [P.sb("w2s0", [128, 8 * D], BF16), P.sb("w2s1", [128, 8 * D], BF16)]
    lg = P.sb("lg", [128, 8])
    sg8 = P.sb("sg8", [128, 8])
    eq8 = P.sb("eq8", [128, 8])
    msk8 = P.sb("msk8", [128, 8])
    ind8 = P.sb("ind8", [128, 8])
    w8 = P.sb("w8", [128, 8])
    we_all = P.sb("we_all", [128, HT * 9])
    rhbf = P.sb("rhbf", [128, 128], BF16)
    hid = P.sb("hid", [128, 8 * 128], BF16)
    outt = [P.sb("outt0", [128, D]), P.sb("outt1", [128, D])]

    pt = P.psum("pt", [128, D], BF16)
    pt32 = P.psum("pt32", [128, D])
    ph = [P.psum("ph0", [128, 128]), P.psum("ph1", [128, 128])]
    po = P.psum("po", [128, D])
    pg = P.psum("pg", [128, 8])

    LD, WL, DV, AC, TE, ST = "LD", "WL", "DV", "AC", "TE", "ST"
    for s in (LD, WL, DV, AC, TE, ST):
        P.sem(s)

    xv = x.rearrange("(t p) d -> p t d", p=128)
    xov = xo.rearrange("(t p) d -> p t d", p=128)
    w1v = w1t.rearrange("(e c p) h -> e p c h", e=9, p=128)
    w2v = w2t.rearrange("(e c p) h -> e p c h", e=9, p=128)

    t_id = P.do("gpsimd", lambda e: e.dma_start(out=idb[:], in_=ident[:]), inc=LD, k=16)
    t_idf = P.do("sync", lambda e: e.dma_start(out=idf[:], in_=ident[:]), inc=LD, k=16)
    t_gw = P.do("sync", lambda e: e.dma_start(out=gws[:].rearrange("p (c n) -> p c n", c=8),
                                              in_=gwt.rearrange("(c p) n -> p c n", p=128)), inc=LD, k=16)
    t_lb = P.do("sync", lambda e: e.dma_start(out=lbrow[:], in_=lbr[:]), inc=LD, k=16)
    t_ones = P.do("vector", lambda e: e.memset(ones_row[:], 1.0), inc=DV)
    t_eps = P.do("vector", lambda e: e.memset(epsb[:], 1e-5), inc=DV)
    tb = P.do("tensor", lambda e: e.matmul(pg[:, :], ones_row[0:1, :], lbrow[0:1, :],
                                           start=True, stop=True), deps=[t_lb, t_ones], inc=TE)
    t_lbb = P.do("vector", lambda e: e.tensor_copy(lbb[:], pg[:, :]), deps=[tb], inc=DV)

    def emit_ln_stats(src_ap, deps):
        m1 = P.do("vector", lambda e: e.reduce_sum(st1[:, 0:1], src_ap, AX.X), deps=deps, inc=DV)
        q1 = P.do("scalar", lambda e: e.activation(sScr[:], src_ap, AF.Square, accum_out=st1[:, 1:2]),
                  deps=deps, inc=AC)
        m1 = P.do("vector", lambda e: e.tensor_scalar_mul(st1[:, 0:1], st1[:, 0:1], 1.0 / D), deps=[m1], inc=DV)
        m2 = P.do("vector", lambda e: e.tensor_mul(st1[:, 2:3], st1[:, 0:1], st1[:, 0:1]), deps=[m1], inc=DV)
        vv = P.do("vector", lambda e: e.scalar_tensor_tensor(st1[:, 3:4], st1[:, 1:2], 1.0 / D, st1[:, 2:3],
                                                             ALU.mult, ALU.subtract), deps=[m2, q1], inc=DV)
        lv = P.do("scalar", lambda e: e.activation(st1[:, 3:4], st1[:, 3:4], AF.Ln, bias=epsb[:, 0:1]),
                  deps=[vv, t_eps], inc=AC)
        return P.do("scalar", lambda e: e.activation(st1[:, 4:5], st1[:, 3:4], AF.Exp, scale=-0.5),
                    deps=[lv], inc=AC)

    xl_free = [None, None]
    st_toks = [None, None]
    wfree = [None, None]

    for half in range(2):
        t0 = half * HT
        ptq_free = None
        pt32_free = None
        pg_free = None
        gate_done = None
        yfm_done = None
        sy_free = None
        for tl in range(HT):
            t = t0 + tl
            par = tl % 2
            lx = P.do("sync", lambda e, t=t, par=par: e.dma_start(out=xtl[par][:], in_=xv[:, t, :]),
                      deps=[xl_free[par]], inc=LD, k=16)
            rs = emit_ln_stats(xtl[par][:], [lx])
            ty = P.do("vector", lambda e, par=par: e.tensor_scalar(
                sY[:], xtl[par][:], st1[:, 0:1], st1[:, 4:5], ALU.subtract, ALU.mult),
                deps=[rs, sy_free], inc=DV)
            xl_free[par] = ty
            tc = P.do("vector", lambda e: e.tensor_copy(sBF[:], sY[:]), deps=[ty], inc=DV)
            tts = None
            for c in range(8):
                tts = P.do("tensor", lambda e, c=c: e.transpose(
                    pt[:, c * 128:(c + 1) * 128], sBF[:, c * 128:(c + 1) * 128], idb[:]),
                    deps=[tc, t_id, ptq_free], inc=TE)
            tev = P.do("vector", lambda e, tl=tl: e.tensor_copy(
                yfm.rearrange("p (c k) -> p c k", c=8)[:, :, tl * 128:(tl + 1) * 128],
                pt.rearrange("p (c k) -> p c k", c=8)[:, :, :]),
                deps=[tts], inc=DV)
            ptq_free = tev
            yfm_done = tev
            tt32 = None
            for c in range(8):
                tt32 = P.do("tensor", lambda e, c=c: e.transpose(
                    pt32[:, c * 128:(c + 1) * 128], sY[:, c * 128:(c + 1) * 128], idf[:]),
                    deps=[tc, t_idf, pt32_free], inc=TE)
            tev32 = P.do("vector", lambda e: e.tensor_copy(yf32[:], pt32[:, :]), deps=[tt32], inc=DV)
            gm = None
            for c in range(8):
                gm = P.do("tensor", lambda e, c=c: e.matmul(
                    pg[:, :], yf32[:, c * 128:(c + 1) * 128],
                    gws[:, c * 8:(c + 1) * 8], start=(c == 0), stop=(c == 7)),
                    deps=[tev32, t_gw, t_lbb, pg_free], inc=TE)
            pt32_free = gm
            sy_free = tt32
            gv = P.do("vector", lambda e: e.tensor_copy(lg[:], pg[:, :]), deps=[gm], inc=DV)
            pg_free = gv
            a = P.do("vector", lambda e: e.tensor_add(msk8[:], lg[:], lbb[:]), deps=[gv, t_lbb], inc=DV)
            a = P.do("vector", lambda e: e.reduce_max(st1[:, 5:6], msk8[:], AX.X), deps=[a], inc=DV)
            a = P.do("vector", lambda e: e.tensor_scalar(eq8[:], msk8[:], st1[:, 5:6], None, ALU.is_ge),
                     deps=[a], inc=DV)
            a = P.do("vector", lambda e: e.scalar_tensor_tensor(msk8[:], eq8[:], -1e30, msk8[:],
                                                                ALU.mult, ALU.add), deps=[a], inc=DV)
            a = P.do("vector", lambda e: e.reduce_max(st1[:, 6:7], msk8[:], AX.X), deps=[a], inc=DV)
            a2 = P.do("vector", lambda e: e.tensor_add(eq8[:], lg[:], lbb[:]), deps=[a], inc=DV)
            ti = P.do("vector", lambda e: e.tensor_scalar(ind8[:], eq8[:], st1[:, 6:7], None, ALU.is_ge),
                      deps=[a2], inc=DV)
            sgm = P.do("scalar", lambda e: e.activation(sg8[:], lg[:], AF.Sigmoid), deps=[gv, ti], inc=AC)
            a = P.do("vector", lambda e: e.tensor_mul(w8[:], sg8[:], ind8[:]), deps=[sgm, ti], inc=DV)
            a = P.do("vector", lambda e: e.reduce_sum(st1[:, 5:6], w8[:], AX.X), deps=[a], inc=DV)
            a = P.do("vector", lambda e: e.tensor_scalar_add(st1[:, 5:6], st1[:, 5:6], 1e-6), deps=[a], inc=DV)
            a = P.do("vector", lambda e: e.reciprocal(st1[:, 6:7], st1[:, 5:6]), deps=[a], inc=DV)
            a = P.do("vector", lambda e, tl=tl: e.tensor_scalar_mul(
                we_all[:, tl * 9:tl * 9 + 8], w8[:], st1[:, 6:7]), deps=[a], inc=DV)
            gate_done = P.do("vector", lambda e, tl=tl: e.memset(we_all[:, tl * 9 + 8:tl * 9 + 9], 1.0),
                             deps=[a], inc=DV)

        acc_done = None
        ph_free = [None, None]
        po_free = None
        for ei in range(9):
            b = ei % 2
            l1 = P.do("gpsimd", lambda e, ei=ei, b=b: e.dma_start(
                out=w1s[b][:].rearrange("p (c h) -> p c h", c=8), in_=w1v[ei]),
                deps=[wfree[b]], inc=WL, k=16)
            wld = P.do("gpsimd", lambda e, ei=ei, b=b: e.dma_start(
                out=w2s[b][:].rearrange("p (c h) -> p c h", c=8), in_=w2v[ei]),
                deps=[], inc=WL, k=16)
            last_down_ev = None
            for tl in range(HT):
                hid_done = None
                for hc in range(8):
                    pp = hc % 2
                    mm = None
                    for c in range(8):
                        mm = P.do("tensor", lambda e, c=c, hc=hc, tl=tl, b=b, pp=pp: e.matmul(
                            ph[pp][:, :], w1s[b][:, c * D + hc * 128:c * D + (hc + 1) * 128],
                            yfm[:, c * TOKC + tl * 128:c * TOKC + (tl + 1) * 128],
                            start=(c == 0), stop=(c == 7)),
                            deps=([wld, yfm_done, ph_free[pp]] if c == 0 else []), inc=TE)
                    rl = P.do("scalar", lambda e, pp=pp: e.activation(rhbf[:], ph[pp][:, :], AF.Relu),
                              deps=[mm, hid_done], inc=AC)
                    hid_done = P.do("vector", lambda e, hc=hc: e.tensor_mul(
                        hid[:, hc * 128:(hc + 1) * 128], rhbf[:], rhbf[:]), deps=[rl], inc=DV)
                    ph_free[pp] = hid_done
                mmd = None
                for hc in range(8):
                    for hh in range(2):
                        mmd = P.do("tensor", lambda e, hc=hc, hh=hh, b=b: e.matmul(
                            po[:, hh * 512:(hh + 1) * 512], hid[:, hc * 128:(hc + 1) * 128],
                            w2s[b][:, hc * D + hh * 512:hc * D + (hh + 1) * 512],
                            start=(hc == 0), stop=(hc == 7)),
                            deps=([hid_done, po_free] if hc == 0 and hh == 0 else []), inc=TE)
                if ei == 0:
                    ev = P.do("vector", lambda e, tl=tl: e.tensor_scalar_mul(
                        acc[:, tl * D:(tl + 1) * D], po[:, :], we_all[:, tl * 9:tl * 9 + 1]),
                        deps=[mmd, gate_done], inc=DV)
                else:
                    ev = P.do("vector", lambda e, tl=tl, ei=ei: e.scalar_tensor_tensor(
                        acc[:, tl * D:(tl + 1) * D], po[:, :], we_all[:, tl * 9 + ei:tl * 9 + ei + 1],
                        acc[:, tl * D:(tl + 1) * D], ALU.mult, ALU.add),
                        deps=[mmd, acc_done], inc=DV)
                po_free = ev
                last_down_ev = ev
            acc_done = last_down_ev
            wfree[b] = last_down_ev

        for tl in range(HT):
            t = t0 + tl
            par = tl % 2
            lx = P.do("sync", lambda e, t=t, par=par: e.dma_start(out=xtl[par][:], in_=xv[:, t, :]),
                      deps=[xl_free[par]], inc=LD, k=16)
            tf = P.do("vector", lambda e, tl=tl, par=par: e.tensor_add(
                outt[par][:], acc[:, tl * D:(tl + 1) * D], xtl[par][:]),
                deps=[lx, acc_done, st_toks[par]], inc=DV)
            xl_free[par] = tf
            st_toks[par] = P.do("gpsimd", lambda e, t=t, par=par: e.dma_start(
                out=xov[:, t, :], in_=outt[par][:]), deps=[tf], inc=ST, k=16)

    P.wait("gpsimd", *P.dma_done("ST"))
    return P.finalize()


def prep_l0_ffn(d):
    w1 = d["L0_ffn_w1"]; w2 = d["L0_ffn_w2"]
    cfc = d["L0_ffn_cfc"]; cproj = d["L0_ffn_cproj"]
    w1t = np.concatenate([np.ascontiguousarray(w1[e].T) for e in range(8)]
                         + [np.ascontiguousarray(cfc.T)], axis=0)
    w2t = np.concatenate([np.ascontiguousarray(w2[e].T) for e in range(8)]
                         + [np.ascontiguousarray(cproj.T)], axis=0)
    return {
        "w1t": w1t, "w2t": w2t,
        "gwt": np.ascontiguousarray(d["L0_ffn_gate_w"].T),
        "lbr": np.ascontiguousarray(d["L0_ffn_lb_bias"][None, :]),
        "ident": np.eye(128, dtype=np.float32),
    }


_L0_FFN_NC = None


def launch_l0_ffn(xflat, d):
    global _L0_FFN_NC
    if _L0_FFN_NC is None:
        _L0_FFN_NC = build_l0_ffn()
    com = prep_l0_ffn(d)
    in_maps = []
    for c in range(NCORE):
        m = dict(com)
        m["x"] = np.ascontiguousarray(xflat[c * TOK:(c + 1) * TOK])
        in_maps.append(m)
    res = run_bass_kernel_spmd(_L0_FFN_NC, in_maps, core_ids=list(range(NCORE)))
    return np.concatenate([res.results[c]["xo"] for c in range(NCORE)], axis=0)



def build_l1_attn():
    P = PB()
    NKT = 17
    XKR = NKT * 128
    x = P.din("xk", [XKR, D])
    xg = P.din("xg", [128, D])
    wq = P.din("wq", [D, D])
    wk = P.din("wk", [D, D])
    wv = P.din("wv", [D, D])
    wo = P.din("wo", [D, D])
    cosk = P.din("cosk", [128, XKR])
    sink = P.din("sink", [128, XKR])
    coskg = P.din("coskg", [128, 128])
    sinkg = P.din("sinkg", [128, 128])
    dA = P.din("dA", [128, 128])
    dB = P.din("dB", [128, 128])
    dA0 = P.din("dA0", [128, 128])
    rt = P.din("rt", [128, 128])
    ident = P.din("ident", [128, 128])
    xo = P.dout("xo", [TOK, D])

    SC = 0.125
    SLOPES = [2.0 ** (-(h + 1) * 0.5) for h in range(16)]
    CHI = 560.0
    CLO = -696.0

    xtl = [P.sb("xtl0", [128, D]), P.sb("xtl1", [128, D])]
    idb = P.sb("idb", [128, 128], BF16)
    rtb = P.sb("rtb", [128, 128], BF16)
    wqs = P.sb("wqs", [128, 8 * D], BF16)
    wks = P.sb("wks", [128, 8 * D], BF16)
    wvs = P.sb("wvs", [128, 8 * D], BF16)
    wos = P.sb("wos", [128, 8 * D], BF16)
    ck = P.sb("ck", [128, XKR], BF16)
    sk = P.sb("sk", [128, XKR], BF16)
    ckg = P.sb("ckg", [128, 128], BF16)
    skg = P.sb("skg", [128, 128], BF16)
    dAs = P.sb("dAs", [128, 128])
    dBs = P.sb("dBs", [128, 128])
    dA0s = P.sb("dA0s", [128, 128])
    epsb = P.sb("epsb", [128, 1])
    st1 = P.sb("st1", [128, 8])
    sScr = P.sb("sScr", [128, D])
    sY = P.sb("sY", [128, D])
    sBF = P.sb("sBF", [128, D], BF16)
    yfm = P.sb("yfm", [128, 8 * 512], BF16)
    kfm = P.sb("kfm", [128, 8 * XKR], BF16)
    kgfm = P.sb("kgfm", [128, 8 * 128], BF16)
    qfm = P.sb("qfm", [128, 8 * 512], BF16)
    vtm = P.sb("vtm", [128, NKT * 16 * 65], BF16)
    vgtm = P.sb("vgtm", [128, 16 * 65], BF16)
    qb = P.sb("qb", [128, 512], BF16)
    t1 = P.sb("t1", [128, 512])
    t2 = P.sb("t2", [128, 512])
    sMSK = P.sb("sMSK", [128, 128])
    eBA = P.sb("eBA", [128, 128], BF16)
    eBB = P.sb("eBB", [128, 128], BF16)
    eBG = P.sb("eBG", [128, 128], BF16)
    sOL = P.sb("sOL", [65, 128])
    sOG = P.sb("sOG", [65, 128])
    sNL = P.sb("sNL", [1, 128])
    sNG = P.sb("sNG", [1, 128])
    ones64 = P.sb("ones64", [1, 64])
    ofm = P.sb("ofm", [128, 8 * 512], BF16)
    outt = [P.sb("outt0", [128, D]), P.sb("outt1", [128, D])]

    pt = P.psum("pt", [128, D], BF16)
    pproj = P.psum("pproj", [128, 512])
    pmix = P.psum("pmix", [128, 512])
    psc = [P.psum("psc0", [128, 128]), P.psum("psc1", [128, 128])]
    pol = P.psum("pol", [128, 128])
    pog = P.psum("pog", [128, 128])

    LD, DV, AC, TE, ST = "LD", "DV", "AC", "TE", "ST"
    for s in (LD, DV, AC, TE, ST):
        P.sem(s)

    xv = x.rearrange("(t p) d -> p t d", p=128)
    xov = xo.rearrange("(t p) d -> p t d", p=128)
    kfm_v = kfm[:].rearrange("p (c k) -> p c k", c=8)
    kgfm_v = kgfm[:].rearrange("p (c k) -> p c k", c=8)
    qfm_v = qfm[:].rearrange("p (c k) -> p c k", c=8)
    yfm_v = yfm[:].rearrange("p (c k) -> p c k", c=8)
    yk_v = yfm_v[:, :, 0:128]
    sT1 = t1[0:64, 128:256]
    sT2 = t1[0:64, 256:384]
    ofm_v = ofm[:].rearrange("p (c k) -> p c k", c=8)
    vtm_v = vtm[:].rearrange("p (t g k) -> p t g k", t=NKT, g=16)
    vgtm_v = vgtm[:].rearrange("p (g k) -> p g k", g=16)

    t_w = None
    for (dst, srcw) in ((wqs, wq), (wks, wk), (wvs, wv), (wos, wo)):
        t_w = P.do("gpsimd", lambda e, dst=dst, srcw=srcw: e.dma_start(
            out=dst[:].rearrange("p (c n) -> p c n", c=8),
            in_=srcw.rearrange("(c p) n -> p c n", p=128)), inc=LD, k=16)
    t_id = P.do("gpsimd", lambda e: e.dma_start(out=idb[:], in_=ident[:]), inc=LD, k=16)
    t_rt = P.do("gpsimd", lambda e: e.dma_start(out=rtb[:], in_=rt[:]), inc=LD, k=16)
    t_tabs = None
    for (dst, srct) in ((ck, cosk), (sk, sink), (ckg, coskg), (skg, sinkg)):
        t_tabs = P.do("gpsimd", lambda e, dst=dst, srct=srct: e.dma_start(out=dst[:], in_=srct[:]), inc=LD, k=16)
    t_tb = None
    for (dst, srct) in ((dAs, dA), (dBs, dB), (dA0s, dA0)):
        t_tb = P.do("sync", lambda e, dst=dst, srct=srct: e.dma_start(out=dst[:], in_=srct[:]), inc=LD, k=16)
    t_eps = P.do("vector", lambda e: e.memset(epsb[:], 1e-5), inc=DV)
    t_v1 = P.do("vector", lambda e: e.memset(vtm[:], 1.0), inc=DV)
    t_vg1 = P.do("vector", lambda e: e.memset(vgtm[:], 1.0), inc=DV)
    t_o64 = P.do("vector", lambda e: e.memset(ones64[:], 1.0), inc=DV)

    def emit_ln_stats(src_ap, deps):
        m1 = P.do("vector", lambda e: e.reduce_sum(st1[:, 0:1], src_ap, AX.X), deps=deps, inc=DV)
        q1 = P.do("scalar", lambda e: e.activation(sScr[:], src_ap, AF.Square, accum_out=st1[:, 1:2]),
                  deps=deps, inc=AC)
        m1 = P.do("vector", lambda e: e.tensor_scalar_mul(st1[:, 0:1], st1[:, 0:1], 1.0 / D), deps=[m1], inc=DV)
        m2 = P.do("vector", lambda e: e.tensor_mul(st1[:, 2:3], st1[:, 0:1], st1[:, 0:1]), deps=[m1], inc=DV)
        vv = P.do("vector", lambda e: e.scalar_tensor_tensor(st1[:, 3:4], st1[:, 1:2], 1.0 / D, st1[:, 2:3],
                                                             ALU.mult, ALU.subtract), deps=[m2, q1], inc=DV)
        lv = P.do("scalar", lambda e: e.activation(st1[:, 3:4], st1[:, 3:4], AF.Ln, bias=epsb[:, 0:1]),
                  deps=[vv, t_eps], inc=AC)
        return P.do("scalar", lambda e: e.activation(st1[:, 4:5], st1[:, 3:4], AF.Exp, scale=-0.5),
                    deps=[lv], inc=AC)

    xl_free = [None, None]
    pt_free = None
    sy_free = None
    sbf_free = None

    def emit_y_fm(src_view, par, dst_v, tokw):
        nonlocal pt_free, sy_free, sbf_free
        lx = P.do("sync", lambda e, par=par: e.dma_start(out=xtl[par][:], in_=src_view),
                  deps=[xl_free[par]], inc=LD, k=16)
        rs = emit_ln_stats(xtl[par][:], [lx])
        ty = P.do("vector", lambda e, par=par: e.tensor_scalar(
            sY[:], xtl[par][:], st1[:, 0:1], st1[:, 4:5], ALU.subtract, ALU.mult),
            deps=[rs, sy_free], inc=DV)
        xl_free[par] = ty
        tc = P.do("vector", lambda e: e.tensor_copy(sBF[:], sY[:]), deps=[ty, sbf_free], inc=DV)
        sy_free = tc
        tts = None
        for c in range(8):
            tts = P.do("tensor", lambda e, c=c: e.transpose(
                pt[:, c * 128:(c + 1) * 128], sBF[:, c * 128:(c + 1) * 128], idb[:]),
                deps=[tc, t_id, pt_free], inc=TE)
        sbf_free = tts
        tev = P.do("vector", lambda e, dst_v=dst_v, tokw=tokw: e.tensor_copy(
            dst_v[:, :, tokw * 128:(tokw + 1) * 128],
            pt.rearrange("p (c k) -> p c k", c=8)[:, :, :]),
            deps=[tts], inc=DV)
        pt_free = tev
        return tev

    kb_free = []
    pm_free = None
    ppr_free = None

    def emit_k_rope(y_v, wsb, ctab, stab, coff, kdst_v, kt_off, ydep, width):
        nonlocal kb_free, pm_free, ppr_free
        last = None
        for c in range(8):
            mm = None
            for ci in range(8):
                mm = P.do("tensor", lambda e, c=c, ci=ci: e.matmul(
                    pproj[:, 0:width], wsb[:, ci * D + c * 128:ci * D + (c + 1) * 128],
                    y_v[:, ci, 0:width], start=(ci == 0), stop=(ci == 7)),
                    deps=([ydep, t_w, ppr_free] if ci == 0 else []), inc=TE)
            cb = P.do("scalar", lambda e: e.copy(qb[:, 0:width], pproj[:, 0:width]),
                      deps=[mm] + kb_free, inc=AC)
            ppr_free = cb
            rmm = P.do("tensor", lambda e: e.matmul(pmix[:, 0:width], rtb[:], qb[:, 0:width],
                                                    start=True, stop=True),
                       deps=[cb, t_rt, pm_free], inc=TE)
            a = P.do("vector", lambda e, c=c: e.tensor_tensor(
                t1[:, 0:width], qb[:, 0:width], ctab[:, coff:coff + width], ALU.mult),
                deps=[rmm, t_tabs], inc=DV)
            a2 = P.do("vector", lambda e: e.tensor_tensor(
                t2[:, 0:width], pmix[:, 0:width], stab[:, coff:coff + width], ALU.mult),
                deps=[a], inc=DV)
            kb_free = [a]
            pm_free = a2
            last = P.do("vector", lambda e, c=c: e.tensor_add(
                kdst_v[:, c, kt_off:kt_off + width], t1[:, 0:width], t2[:, 0:width]),
                deps=[a2], inc=DV)
        return last

    def emit_v(y_v, vdst_3d, ydep, deps_extra):
        nonlocal ppr_free
        vev = None
        for hh in range(2):
            mm = None
            for ci in range(8):
                mm = P.do("tensor", lambda e, ci=ci, hh=hh: e.matmul(
                    pproj[:, 0:512], y_v[:, ci, 0:128],
                    wvs[:, ci * D + hh * 512:ci * D + (hh + 1) * 512],
                    start=(ci == 0), stop=(ci == 7)),
                    deps=([ydep, ppr_free] + deps_extra if ci == 0 else []), inc=TE)
            vev = P.do("vector", lambda e, hh=hh, vdst_3d=vdst_3d: e.tensor_copy(
                vdst_3d[:, hh * 8:(hh + 1) * 8, 0:64],
                pproj[:, 0:512].rearrange("p (g k) -> p g k", g=8)),
                deps=[mm], inc=DV)
            ppr_free = vev
        return vev

    kv_done = None
    for kt in range(NKT):
        ydep = emit_y_fm(xv[:, kt, :], kt % 2, yk_v, 0)
        kdep = emit_k_rope(yk_v, wks, ck, sk, kt * 128, kfm_v, kt * 128, ydep, 128)
        kv_done = emit_v(yk_v, vtm_v[:, kt], ydep, [t_v1])
    ydep = emit_y_fm(xg[:, :], 0, yk_v, 0)
    kg_done = emit_k_rope(yk_v, wks, ckg, skg, 0, kgfm_v, 0, ydep, 128)
    vg_done = emit_v(yk_v, vgtm_v, ydep, [t_vg1])

    eBA_free = eBB_free = eBG_free = None
    pscf = [None, None]
    pol_free = None
    pog_free = None
    sol_free = None
    sog_free = None
    msk_free = None
    st_toks = [None, None]
    for g in range(4):
        ydep = None
        for tl in range(4):
            ydep = emit_y_fm(xv[:, 4 * g + tl + 1, :], tl % 2, yfm_v, tl)
        qdep = emit_k_rope(yfm_v, wqs, ck, sk, 128 + g * 512, qfm_v, 0, ydep, 512)
        head_done = []
        for h in range(16):
            hb = (h % 2) * 64
            hc = h // 2
            for qt in range(4):
                gqt = 4 * g + qt
                dA_use = dA0s if gqt == 0 else dAs
                mA = P.do("tensor", lambda e, hb=hb, hc=hc, gqt=gqt, qt=qt: e.matmul(
                    psc[0][:, :], kfm_v[hb:hb + 64, hc, gqt * 128:(gqt + 1) * 128],
                    qfm_v[hb:hb + 64, hc, qt * 128:(qt + 1) * 128],
                    start=True, stop=True), deps=[qdep, kv_done, pscf[0]], inc=TE)
                sA = P.do("vector", lambda e, h=h, dA_use=dA_use: e.scalar_tensor_tensor(
                    sMSK[:], dA_use[:], SLOPES[h] / SC, psc[0][:, :], ALU.mult, ALU.add),
                    deps=[mA, msk_free, t_tb], inc=DV)
                pscf[0] = sA
                sA = P.do("vector", lambda e: e.tensor_scalar(sMSK[:], sMSK[:], CHI, CLO,
                                                              ALU.min, ALU.max), deps=[sA], inc=DV)
                xA = P.do("scalar", lambda e: e.activation(eBA[:], sMSK[:], AF.Exp, scale=SC),
                          deps=[sA, eBA_free], inc=AC)
                mB = P.do("tensor", lambda e, hb=hb, hc=hc, gqt=gqt, qt=qt: e.matmul(
                    psc[1][:, :], kfm_v[hb:hb + 64, hc, (gqt + 1) * 128:(gqt + 2) * 128],
                    qfm_v[hb:hb + 64, hc, qt * 128:(qt + 1) * 128],
                    start=True, stop=True), deps=[pscf[1]], inc=TE)
                sB = P.do("vector", lambda e, h=h: e.scalar_tensor_tensor(
                    sMSK[:], dBs[:], SLOPES[h] / SC, psc[1][:, :], ALU.mult, ALU.add),
                    deps=[mB, xA, t_tb], inc=DV)
                pscf[1] = sB
                sB = P.do("vector", lambda e: e.tensor_scalar(sMSK[:], sMSK[:], CHI, CLO,
                                                              ALU.min, ALU.max), deps=[sB], inc=DV)
                xB = P.do("scalar", lambda e: e.activation(eBB[:], sMSK[:], AF.Exp, scale=SC),
                          deps=[sB, eBB_free], inc=AC)
                msk_free = xB
                mG = P.do("tensor", lambda e, hb=hb, hc=hc, qt=qt: e.matmul(
                    pog[:, :], kgfm_v[hb:hb + 64, hc, :],
                    qfm_v[hb:hb + 64, hc, qt * 128:(qt + 1) * 128],
                    start=True, stop=True), deps=[kg_done, pog_free], inc=TE)
                gclamp = P.do("vector", lambda e: e.tensor_scalar(t1[:, 0:128], pog[:, :], CHI, CLO,
                                                                  ALU.min, ALU.max), deps=[mG], inc=DV)
                pog_free = gclamp
                xG = P.do("scalar", lambda e: e.activation(eBG[:], t1[:, 0:128], AF.Exp, scale=SC),
                          deps=[gclamp, eBG_free], inc=AC)
                oA = P.do("tensor", lambda e, gqt=gqt, h=h: e.matmul(
                    pol[0:65, :], vtm_v[:, gqt, h, 0:65], eBA[:],
                    start=True, stop=False), deps=[xA, pol_free], inc=TE)
                eBA_free = oA
                oB = P.do("tensor", lambda e, gqt=gqt, h=h: e.matmul(
                    pol[0:65, :], vtm_v[:, gqt + 1, h, 0:65], eBB[:],
                    start=False, stop=True), deps=[xB], inc=TE)
                eBB_free = oB
                oG = P.do("tensor", lambda e, h=h: e.matmul(
                    pog[0:65, :], vgtm_v[:, h, 0:65], eBG[:],
                    start=True, stop=True), deps=[xG, gclamp], inc=TE)
                eBG_free = oG
                pog_free = oG
                evL = P.do("vector", lambda e: e.tensor_copy(sOL[:], pol[0:65, :]),
                           deps=[oB, sol_free], inc=DV)
                pol_free = evL
                evG = P.do("vector", lambda e: e.tensor_copy(sOG[:], pog[0:65, :]),
                           deps=[oG, sog_free], inc=DV)
                pog_free = evG
                rL = P.do("vector", lambda e: e.reciprocal(sNL[:], sOL[64:65, :]), deps=[evL], inc=DV)
                rG = P.do("vector", lambda e: e.reciprocal(sNG[:], sOG[64:65, :]), deps=[evG], inc=DV)
                bL = P.do("tensor", lambda e: e.matmul(pmix[0:64, 0:128], ones64[0:1, :], sNL[0:1, :],
                                                       start=True, stop=True),
                          deps=[rL, pm_free, t_o64], inc=TE)
                bG = P.do("tensor", lambda e: e.matmul(pmix[0:64, 128:256], ones64[0:1, :], sNG[0:1, :],
                                                       start=True, stop=True), deps=[rG], inc=TE)
                c1 = P.do("vector", lambda e: e.tensor_mul(sT1, sOL[0:64, :], pmix[0:64, 0:128]),
                          deps=[bL, bG], inc=DV)
                sol_free = c1
                c2 = P.do("vector", lambda e: e.tensor_mul(sT2, sOG[0:64, :], pmix[0:64, 128:256]),
                          deps=[c1], inc=DV)
                sog_free = c2
                pm_free = c2
                cc = P.do("vector", lambda e, hb=hb, hc=hc, qt=qt: e.tensor_add(
                    ofm_v[hb:hb + 64, hc, qt * 128:(qt + 1) * 128], sT1, sT2),
                    deps=[c2], inc=DV)
                head_done.append(cc)
        hd = head_done[-1]
        for tl in range(4):
            t = 4 * g + tl
            par = tl % 2
            lx = P.do("sync", lambda e, t=t, par=par: e.dma_start(out=xtl[par][:], in_=xv[:, t + 1, :]),
                      deps=[xl_free[par]], inc=LD, k=16)
            ev2 = None
            for hh in range(2):
                mm = None
                for ci in range(8):
                    mm = P.do("tensor", lambda e, ci=ci, hh=hh, tl=tl: e.matmul(
                        pproj[:, 0:512], ofm_v[:, ci, tl * 128:(tl + 1) * 128],
                        wos[:, ci * D + hh * 512:ci * D + (hh + 1) * 512],
                        start=(ci == 0), stop=(ci == 7)),
                        deps=([hd, ppr_free] if ci == 0 else []), inc=TE)
                ev2 = P.do("vector", lambda e, hh=hh, par=par: e.tensor_add(
                    outt[par][:, hh * 512:(hh + 1) * 512], pproj[:, 0:512],
                    xtl[par][:, hh * 512:(hh + 1) * 512]),
                    deps=[mm, lx, st_toks[par]], inc=DV)
                ppr_free = ev2
            xl_free[par] = ev2
            st_toks[par] = P.do("gpsimd", lambda e, t=t, par=par: e.dma_start(
                out=xov[:, t, :], in_=outt[par][:]), deps=[ev2], inc=ST, k=16)

    P.wait("gpsimd", *P.dma_done("ST"))
    return P.finalize()


def prep_l1_attn(d, xflat):
    DIMH, NH, W = 64, 16, 64
    inv = (1.0 / (10000.0 ** (np.arange(0, DIMH, 2, dtype=np.float32) / DIMH))).astype(np.float32)

    def tables(pos):
        ang = pos.astype(np.float32)[None, :] * inv[:, None]
        c = np.cos(ang).astype(np.float32)
        s = np.sin(ang).astype(np.float32)
        rows_c = np.tile(c, (4, 1))
        rows_s = np.tile(s, (4, 1))
        return np.ascontiguousarray(rows_c), np.ascontiguousarray(rows_s)

    jl = np.arange(128)[:, None]
    il = np.arange(128)[None, :]
    DA = np.where(jl - il >= 64, (jl - 128 - il).astype(np.float32), np.float32(-1e33))
    DBm = np.where((jl <= il) & (jl >= il - 64), (jl - il).astype(np.float32), np.float32(-1e33))
    DINV = np.full((128, 128), -1e33, dtype=np.float32)
    M = np.zeros((128, 128), dtype=np.float32)
    for di in range(128):
        if di % 64 < 32:
            M[di, di + 32] = -1.0
        else:
            M[di, di - 32] = 1.0
    com = {
        "wq": d["L1_attn_Wq"], "wk": d["L1_attn_Wk"], "wv": d["L1_attn_Wv"],
        "wo": d["L1_attn_Wo"],
        "dA": DA.astype(np.float32), "dB": DBm.astype(np.float32),
        "rt": np.ascontiguousarray(M.T), "ident": np.eye(128, dtype=np.float32),
    }
    cg, sg = tables(np.arange(128) * 64)
    com["coskg"], com["sinkg"] = cg, sg
    maps = []
    for c in range(NCORE):
        b, so = c // 4, (c % 4) * 2048
        own = xflat[c * TOK:(c + 1) * TOK]
        halo = (np.zeros((64, D), np.float32) if so == 0
                else xflat[c * TOK - 64:c * TOK])
        xk = np.concatenate([np.zeros((64, D), np.float32), halo, own], axis=0)
        pos = so + np.arange(17 * 128) - 128
        ckt, skt = tables(pos)
        m = dict(com)
        m["xk"] = np.ascontiguousarray(xk)
        m["xg"] = np.ascontiguousarray(xflat[b * 8192:(b + 1) * 8192:64])
        m["cosk"], m["sink"] = ckt, skt
        m["dA0"] = (DINV if so == 0 else DA).astype(np.float32)
        maps.append(m)
    return maps


_L1_ATTN_NC = None


def launch_l1_attn(xflat, d):
    global _L1_ATTN_NC
    if _L1_ATTN_NC is None:
        _L1_ATTN_NC = build_l1_attn()
    in_maps = prep_l1_attn(d, xflat)
    res = run_bass_kernel_spmd(_L1_ATTN_NC, in_maps, core_ids=list(range(NCORE)))
    return np.concatenate([res.results[c]["xo"] for c in range(NCORE)], axis=0)



def build_l1_ffn():
    P = PB()
    HID = 2816
    NCH = 22
    BLKS = [(0, 8), (8, 8), (16, 6)]
    x = P.din("x", [TOK, D])
    w1t = P.din("w1t", [8 * D, HID])
    w2t = P.din("w2t", [8 * D, HID])
    w3t = P.din("w3t", [8 * HID, D])
    gwt = P.din("gwt", [D, 8])
    ident = P.din("ident", [128, 128])
    xo = P.dout("xo", [TOK, D])

    HT = 8
    TOKC = HT * 128

    xtl = [P.sb("xtl0", [128, D]), P.sb("xtl1", [128, D])]
    idb = P.sb("idb", [128, 128], BF16)
    idf = P.sb("idf", [128, 128])
    gws = P.sb("gws", [128, 8 * 8])
    epsb = P.sb("epsb", [128, 1])
    st1 = P.sb("st1", [128, 8])
    sScr = P.sb("sScr", [128, D])
    sY = P.sb("sY", [128, D])
    sBF = P.sb("sBF", [128, D], BF16)
    yf32 = P.sb("yf32", [128, D])
    yfm = P.sb("yfm", [128, 8 * TOKC], BF16)
    acc = P.sb("acc", [128, HT * D])
    w1s = [P.sb("w1s0", [128, 8 * 1024], BF16), P.sb("w1s1", [128, 8 * 1024], BF16)]
    w2s = [P.sb("w2s0", [128, 8 * 1024], BF16), P.sb("w2s1", [128, 8 * 1024], BF16)]
    w3s = [P.sb("w3s0", [128, 8 * 1024], BF16), P.sb("w3s1", [128, 8 * 1024], BF16)]
    lg = P.sb("lg", [128, 8])
    eq8 = P.sb("eq8", [128, 8])
    msk8 = P.sb("msk8", [128, 8])
    ind8 = P.sb("ind8", [128, 8])
    w8 = P.sb("w8", [128, 8])
    we_all = P.sb("we_all", [128, HT * 8])
    ss = P.sb("ss", [128, 512])
    hid = P.sb("hid", [128, 8 * 512], BF16)
    outt = [P.sb("outt0", [128, D]), P.sb("outt1", [128, D])]

    pt = P.psum("pt", [128, D], BF16)
    pt32 = P.psum("pt32", [128, D])
    pg = P.psum("pg", [128, 8])
    ph1 = P.psum("ph1", [128, 512])
    ph2 = P.psum("ph2", [128, 512])
    po = P.psum("po", [128, D])

    LD, WL, DV, AC, TE, ST = "LD", "WL", "DV", "AC", "TE", "ST"
    for s in (LD, WL, DV, AC, TE, ST):
        P.sem(s)

    xv = x.rearrange("(t p) d -> p t d", p=128)
    xov = xo.rearrange("(t p) d -> p t d", p=128)
    w1v = w1t.rearrange("(e c p) h -> e p c h", e=8, p=128)
    w2v = w2t.rearrange("(e c p) h -> e p c h", e=8, p=128)
    w3v = w3t.rearrange("(e c p) n -> e p c n", e=8, p=128)
    yfm_v = yfm[:].rearrange("p (c k) -> p c k", c=8)
    hid_v = hid[:].rearrange("p (c k) -> p c k", c=8)

    t_id = P.do("gpsimd", lambda e: e.dma_start(out=idb[:], in_=ident[:]), inc=LD, k=16)
    t_idf = P.do("sync", lambda e: e.dma_start(out=idf[:], in_=ident[:]), inc=LD, k=16)
    t_gw = P.do("sync", lambda e: e.dma_start(out=gws[:].rearrange("p (c n) -> p c n", c=8),
                                              in_=gwt.rearrange("(c p) n -> p c n", p=128)), inc=LD, k=16)
    t_eps = P.do("vector", lambda e: e.memset(epsb[:], 1e-5), inc=DV)

    def emit_ln_stats(src_ap, deps):
        m1 = P.do("vector", lambda e: e.reduce_sum(st1[:, 0:1], src_ap, AX.X), deps=deps, inc=DV)
        q1 = P.do("scalar", lambda e: e.activation(sScr[:], src_ap, AF.Square, accum_out=st1[:, 1:2]),
                  deps=deps, inc=AC)
        m1 = P.do("vector", lambda e: e.tensor_scalar_mul(st1[:, 0:1], st1[:, 0:1], 1.0 / D), deps=[m1], inc=DV)
        m2 = P.do("vector", lambda e: e.tensor_mul(st1[:, 2:3], st1[:, 0:1], st1[:, 0:1]), deps=[m1], inc=DV)
        vv = P.do("vector", lambda e: e.scalar_tensor_tensor(st1[:, 3:4], st1[:, 1:2], 1.0 / D, st1[:, 2:3],
                                                             ALU.mult, ALU.subtract), deps=[m2, q1], inc=DV)
        lv = P.do("scalar", lambda e: e.activation(st1[:, 3:4], st1[:, 3:4], AF.Ln, bias=epsb[:, 0:1]),
                  deps=[vv, t_eps], inc=AC)
        return P.do("scalar", lambda e: e.activation(st1[:, 4:5], st1[:, 3:4], AF.Exp, scale=-0.5),
                    deps=[lv], inc=AC)

    xl_free = [None, None]
    st_toks = [None, None]
    wfree = {}

    for half in range(2):
        t0 = half * HT
        pt_free = None
        pt32_free = None
        pg_free = None
        sy_free = None
        sbf_free = None
        gate_done = None
        yfm_done = None
        for tl in range(HT):
            t = t0 + tl
            par = tl % 2
            lx = P.do("sync", lambda e, t=t, par=par: e.dma_start(out=xtl[par][:], in_=xv[:, t, :]),
                      deps=[xl_free[par]], inc=LD, k=16)
            rs = emit_ln_stats(xtl[par][:], [lx])
            ty = P.do("vector", lambda e, par=par: e.tensor_scalar(
                sY[:], xtl[par][:], st1[:, 0:1], st1[:, 4:5], ALU.subtract, ALU.mult),
                deps=[rs, sy_free], inc=DV)
            xl_free[par] = ty
            tc = P.do("vector", lambda e: e.tensor_copy(sBF[:], sY[:]), deps=[ty, sbf_free], inc=DV)
            tts = None
            for c in range(8):
                tts = P.do("tensor", lambda e, c=c: e.transpose(
                    pt[:, c * 128:(c + 1) * 128], sBF[:, c * 128:(c + 1) * 128], idb[:]),
                    deps=[tc, t_id, pt_free], inc=TE)
            sbf_free = tts
            tev = P.do("vector", lambda e, tl=tl: e.tensor_copy(
                yfm_v[:, :, tl * 128:(tl + 1) * 128],
                pt.rearrange("p (c k) -> p c k", c=8)[:, :, :]),
                deps=[tts], inc=DV)
            pt_free = tev
            yfm_done = tev
            tt32 = None
            for c in range(8):
                tt32 = P.do("tensor", lambda e, c=c: e.transpose(
                    pt32[:, c * 128:(c + 1) * 128], sY[:, c * 128:(c + 1) * 128], idf[:]),
                    deps=[tc, t_idf, pt32_free], inc=TE)
            sy_free = tt32
            tev32 = P.do("vector", lambda e: e.tensor_copy(yf32[:], pt32[:, :]), deps=[tt32], inc=DV)
            gm = None
            for c in range(8):
                gm = P.do("tensor", lambda e, c=c: e.matmul(
                    pg[:, :], yf32[:, c * 128:(c + 1) * 128],
                    gws[:, c * 8:(c + 1) * 8], start=(c == 0), stop=(c == 7)),
                    deps=[tev32, t_gw, pg_free], inc=TE)
            pt32_free = gm
            gv = P.do("vector", lambda e: e.tensor_copy(lg[:], pg[:, :]), deps=[gm], inc=DV)
            pg_free = gv
            a = P.do("vector", lambda e: e.reduce_max(st1[:, 5:6], lg[:], AX.X), deps=[gv], inc=DV)
            a = P.do("vector", lambda e: e.tensor_scalar(eq8[:], lg[:], st1[:, 5:6], None, ALU.is_ge),
                     deps=[a], inc=DV)
            a = P.do("vector", lambda e: e.scalar_tensor_tensor(msk8[:], eq8[:], -1e30, lg[:],
                                                                ALU.mult, ALU.add), deps=[a], inc=DV)
            a = P.do("vector", lambda e: e.reduce_max(st1[:, 6:7], msk8[:], AX.X), deps=[a], inc=DV)
            ti = P.do("vector", lambda e: e.tensor_scalar(ind8[:], lg[:], st1[:, 6:7], None, ALU.is_ge),
                      deps=[a], inc=DV)
            nm = P.do("vector", lambda e: e.tensor_scalar_mul(st1[:, 7:8], st1[:, 5:6], -1.0),
                      deps=[ti], inc=DV)
            ex = P.do("scalar", lambda e: e.activation(msk8[:], lg[:], AF.Exp, bias=st1[:, 7:8]),
                      deps=[nm], inc=AC)
            a = P.do("vector", lambda e: e.tensor_mul(w8[:], msk8[:], ind8[:]), deps=[ex], inc=DV)
            a = P.do("vector", lambda e: e.reduce_sum(st1[:, 5:6], w8[:], AX.X), deps=[a], inc=DV)
            a = P.do("vector", lambda e: e.reciprocal(st1[:, 6:7], st1[:, 5:6]), deps=[a], inc=DV)
            gate_done = P.do("vector", lambda e, tl=tl: e.tensor_scalar_mul(
                we_all[:, tl * 8:tl * 8 + 8], w8[:], st1[:, 6:7]), deps=[a], inc=DV)

        acc_done = None
        ph_free = [None, None]
        po_free = None
        for ei in range(8):
            for bi, (c0, nch) in enumerate(BLKS):
                b = (ei * 3 + bi) % 2
                wld = None
                for (dst, srcv, wd) in ((w1s[b], w1v, 128), (w2s[b], w2v, 128)):
                    wld = P.do("gpsimd", lambda e, dst=dst, srcv=srcv, ei=ei, c0=c0, nch=nch: e.dma_start(
                        out=dst[:].rearrange("p (c h) -> p c h", c=8)[:, :, 0:nch * 128],
                        in_=srcv[ei][:, :, c0 * 128:(c0 + nch) * 128]),
                        deps=[wfree.get(b)], inc=WL, k=16)
                wld = P.do("gpsimd", lambda e, ei=ei, b=b, c0=c0, nch=nch: e.dma_start(
                    out=w3s[b][:, 0:nch * 1024].rearrange("p (c h) -> p c h", c=nch),
                    in_=w3v[ei][:, c0:c0 + nch, :]),
                    deps=[], inc=WL, k=16)
                last_ev = None
                for tg in range(2):
                    qoff = tg * 512
                    hid_done = None
                    for hc in range(nch):
                        pp = hc % 2
                        phx = [ph1, ph2][pp] if False else None
                        mm1 = None
                        for ci in range(8):
                            mm1 = P.do("tensor", lambda e, ci=ci, hc=hc, b=b, qoff=qoff: e.matmul(
                                ph1[:, :], w1s[b][:, ci * 1024 + hc * 128:ci * 1024 + (hc + 1) * 128],
                                yfm_v[:, ci, qoff:qoff + 512], start=(ci == 0), stop=(ci == 7)),
                                deps=([wld, yfm_done, ph_free[0]] if ci == 0 else []), inc=TE)
                        mm2 = None
                        for ci in range(8):
                            mm2 = P.do("tensor", lambda e, ci=ci, hc=hc, b=b, qoff=qoff: e.matmul(
                                ph2[:, :], w2s[b][:, ci * 1024 + hc * 128:ci * 1024 + (hc + 1) * 128],
                                yfm_v[:, ci, qoff:qoff + 512], start=(ci == 0), stop=(ci == 7)),
                                deps=([ph_free[1]] if ci == 0 else []), inc=TE)
                        sl = P.do("scalar", lambda e: e.activation(ss[:], ph1[:, :], AF.Silu),
                                  deps=[mm1, hid_done], inc=AC)
                        ph_free[0] = sl
                        hid_done = P.do("vector", lambda e, hc=hc: e.tensor_tensor(
                            hid_v[:, hc, :], ss[:], ph2[:, :], ALU.mult), deps=[sl, mm2], inc=DV)
                        ph_free[1] = hid_done
                    for tli in range(4):
                        tl = tg * 4 + tli
                        mmd = None
                        for hc in range(nch):
                            for hh in range(2):
                                mmd = P.do("tensor", lambda e, hc=hc, hh=hh, b=b, tli=tli: e.matmul(
                                    po[:, hh * 512:(hh + 1) * 512],
                                    hid_v[:, hc, tli * 128:(tli + 1) * 128],
                                    w3s[b][:, hc * 1024 + hh * 512:hc * 1024 + (hh + 1) * 512],
                                    start=(hc == 0), stop=(hc == nch - 1)),
                                    deps=([hid_done, po_free] if hc == 0 and hh == 0 else []), inc=TE)
                        if ei == 0 and bi == 0:
                            ev = P.do("vector", lambda e, tl=tl: e.tensor_scalar_mul(
                                acc[:, tl * D:(tl + 1) * D], po[:, :], we_all[:, tl * 8:tl * 8 + 1]),
                                deps=[mmd, gate_done], inc=DV)
                        else:
                            ev = P.do("vector", lambda e, tl=tl, ei=ei: e.scalar_tensor_tensor(
                                acc[:, tl * D:(tl + 1) * D], po[:, :], we_all[:, tl * 8 + ei:tl * 8 + ei + 1],
                                acc[:, tl * D:(tl + 1) * D], ALU.mult, ALU.add),
                                deps=[mmd, acc_done], inc=DV)
                        po_free = ev
                        last_ev = ev
                        acc_done = ev
                wfree[b] = last_ev

        for tl in range(HT):
            t = t0 + tl
            par = tl % 2
            lx = P.do("sync", lambda e, t=t, par=par: e.dma_start(out=xtl[par][:], in_=xv[:, t, :]),
                      deps=[xl_free[par]], inc=LD, k=16)
            tf = P.do("vector", lambda e, tl=tl, par=par: e.tensor_add(
                outt[par][:], acc[:, tl * D:(tl + 1) * D], xtl[par][:]),
                deps=[lx, acc_done, st_toks[par]], inc=DV)
            xl_free[par] = tf
            st_toks[par] = P.do("gpsimd", lambda e, t=t, par=par: e.dma_start(
                out=xov[:, t, :], in_=outt[par][:]), deps=[tf], inc=ST, k=16)

    P.wait("gpsimd", *P.dma_done("ST"))
    return P.finalize()


def prep_l1_ffn(d):
    W1 = d["L1_ffn_W1"]; W2 = d["L1_ffn_W2"]; W3 = d["L1_ffn_W3"]
    HID = 2816
    pad_h = HID - W1.shape[1]
    w1t = np.concatenate([np.ascontiguousarray(
        np.pad(W1[e], ((0, pad_h), (0, 0))).T) for e in range(8)], axis=0)
    w2t = np.concatenate([np.ascontiguousarray(
        np.pad(W2[e], ((0, pad_h), (0, 0))).T) for e in range(8)], axis=0)
    w3t = np.concatenate([np.ascontiguousarray(
        np.pad(W3[e], ((0, 0), (0, pad_h))).T) for e in range(8)], axis=0)
    return {
        "w1t": w1t, "w2t": w2t, "w3t": w3t,
        "gwt": np.ascontiguousarray(d["L1_ffn_Wg"].T),
        "ident": np.eye(128, dtype=np.float32),
    }


_L1_FFN_NC = None


def launch_l1_ffn(xflat, d):
    global _L1_FFN_NC
    if _L1_FFN_NC is None:
        _L1_FFN_NC = build_l1_ffn()
    com = prep_l1_ffn(d)
    in_maps = []
    for c in range(NCORE):
        m = dict(com)
        m["x"] = np.ascontiguousarray(xflat[c * TOK:(c + 1) * TOK])
        in_maps.append(m)
    res = run_bass_kernel_spmd(_L1_FFN_NC, in_maps, core_ids=list(range(NCORE)))
    return np.concatenate([res.results[c]["xo"] for c in range(NCORE)], axis=0)



def build_l2_attn():
    P = PB()
    NBT = 64
    x = P.din("x", [TOK, D])
    xf = P.din("xf", [8192, D])
    wqp = P.din("wqp", [512, D])
    wkp = P.din("wkp", [512, D])
    wvt = P.din("wvt", [512, 512])
    wop = P.din("wop", [D, D])
    bo = P.din("bo", [1, D])
    ident = P.din("ident", [128, 128])
    xo = P.dout("xo", [TOK, D])

    xtl = [P.sb("xtl0", [128, D]), P.sb("xtl1", [128, D])]
    idb = P.sb("idb", [128, 128], BF16)
    wqs = P.sb("wqs", [128, 4 * D], BF16)
    wks = P.sb("wks", [128, 4 * D], BF16)
    wvs = P.sb("wvs", [128, 4 * 512], BF16)
    wos = P.sb("wos", [128, 8 * D], BF16)
    bos = P.sb("bos", [1, D])
    bob = P.sb("bob", [128, D])
    ones_row = P.sb("ones_row", [1, 128])
    epsb = P.sb("epsb", [128, 1])
    st1 = P.sb("st1", [128, 8])
    sY = P.sb("sY", [128, 512])
    sBF = P.sb("sBF", [128, D], BF16)
    yfm = P.sb("yfm", [128, 4 * 8192], BF16)
    k2 = P.sb("k2", [128, 8192], BF16)
    v2 = P.sb("v2", [128, NBT * 66], BF16)
    q2 = P.sb("q2", [128, TOK], BF16)
    eB = [P.sb("eB0", [128, 512], BF16), P.sb("eB1", [128, 512], BF16)]
    sO = P.sb("sO", [33, 512])
    sN = P.sb("sN", [1, 512])
    opad = P.sb("opad", [128, 8 * TOK], BF16)
    outt = [P.sb("outt0", [128, D]), P.sb("outt1", [128, D])]

    pt = P.psum("pt", [128, 512], BF16)
    pproj = P.psum("pproj", [128, 512])
    psc = [P.psum("psc0", [128, 512]), P.psum("psc1", [128, 512])]
    pout = P.psum("pout", [33, 512])
    pb = P.psum("pb", [128, 512])

    LD, DV, AC, TE, ST = "LD", "DV", "AC", "TE", "ST"
    for s in (LD, DV, AC, TE, ST):
        P.sem(s)

    xv = x.rearrange("(t p) d -> p t d", p=128)
    xfv = xf.rearrange("(t p) d -> p t d", p=128)
    xov = xo.rearrange("(t p) d -> p t d", p=128)
    yfm_v = yfm[:].rearrange("p (c k) -> p c k", c=4)
    v2_v = v2[:].rearrange("p (t j) -> p t j", t=NBT)
    opad_v = opad[:].rearrange("p (c k) -> p c k", c=8)

    t_w = None
    for (dst, srcw, nch) in ((wqs, wqp, 4), (wks, wkp, 4), (wvs, wvt, 4), (wos, wop, 8)):
        t_w = P.do("gpsimd", lambda e, dst=dst, srcw=srcw, nch=nch: e.dma_start(
            out=dst[:].rearrange("p (c n) -> p c n", c=nch),
            in_=srcw.rearrange("(c p) n -> p c n", p=128)), inc=LD, k=16)
    t_id = P.do("gpsimd", lambda e: e.dma_start(out=idb[:], in_=ident[:]), inc=LD, k=16)
    t_bo = P.do("sync", lambda e: e.dma_start(out=bos[:], in_=bo[:]), inc=LD, k=16)
    t_eps = P.do("vector", lambda e: e.memset(epsb[:], 1e-5), inc=DV)
    t_ones = P.do("vector", lambda e: e.memset(ones_row[:], 1.0), inc=DV)
    t_v1 = P.do("vector", lambda e: e.memset(v2[:], 1.0), inc=DV)
    tb = P.do("tensor", lambda e: e.matmul(pb[:, 0:512], ones_row[0:1, :], bos[0:1, 0:512],
                                           start=True, stop=True), deps=[t_bo, t_ones], inc=TE)
    t_bb = P.do("vector", lambda e: e.tensor_copy(bob[:, 0:512], pb[:, 0:512]), deps=[tb], inc=DV)
    tb2 = P.do("tensor", lambda e: e.matmul(pb[:, 0:512], ones_row[0:1, :], bos[0:1, 512:1024],
                                            start=True, stop=True), deps=[t_bb], inc=TE)
    t_bb = P.do("vector", lambda e: e.tensor_copy(bob[:, 512:1024], pb[:, 0:512]), deps=[tb2], inc=DV)

    def emit_ln_stats(src_ap, deps):
        m1 = P.do("vector", lambda e: e.reduce_sum(st1[:, 0:1], src_ap, AX.X), deps=deps, inc=DV)
        q1 = P.do("scalar", lambda e: e.activation(q2[:, 0:1024], src_ap, AF.Square, accum_out=st1[:, 1:2]),
                  deps=deps, inc=AC)
        m1 = P.do("vector", lambda e: e.tensor_scalar_mul(st1[:, 0:1], st1[:, 0:1], 1.0 / D), deps=[m1], inc=DV)
        m2 = P.do("vector", lambda e: e.tensor_mul(st1[:, 2:3], st1[:, 0:1], st1[:, 0:1]), deps=[m1], inc=DV)
        vv = P.do("vector", lambda e: e.scalar_tensor_tensor(st1[:, 3:4], st1[:, 1:2], 1.0 / D, st1[:, 2:3],
                                                             ALU.mult, ALU.subtract), deps=[m2, q1], inc=DV)
        lv = P.do("scalar", lambda e: e.activation(st1[:, 3:4], st1[:, 3:4], AF.Ln, bias=epsb[:, 0:1]),
                  deps=[vv, t_eps], inc=AC)
        return P.do("scalar", lambda e: e.activation(st1[:, 4:5], st1[:, 3:4], AF.Exp, scale=-0.5),
                    deps=[lv], inc=AC)

    xl_free = [None, None]
    pt_free = None
    sy_free = None
    sbf_free = None
    t_z = P.do("vector", lambda e: e.memset(opad[:], 0.0), inc=DV)

    ydone = None
    for bt in range(NBT):
        par = bt % 2
        lx = P.do("sync", lambda e, bt=bt, par=par: e.dma_start(out=xtl[par][:], in_=xfv[:, bt, :]),
                  deps=[xl_free[par]], inc=LD, k=16)
        rs = emit_ln_stats(xtl[par][:], [lx])
        ty = P.do("vector", lambda e, par=par: e.tensor_scalar(
            sY[:], xtl[par][:, 0:512], st1[:, 0:1], st1[:, 4:5], ALU.subtract, ALU.mult),
            deps=[rs, sy_free], inc=DV)
        xl_free[par] = ty
        tc = P.do("vector", lambda e: e.tensor_copy(sBF[:, 0:512], sY[:]), deps=[ty, sbf_free], inc=DV)
        sy_free = tc
        tts = None
        for c in range(4):
            tts = P.do("tensor", lambda e, c=c: e.transpose(
                pt[:, c * 128:(c + 1) * 128], sBF[:, c * 128:(c + 1) * 128], idb[:]),
                deps=[tc, t_id, pt_free], inc=TE)
        sbf_free = tts
        ydone = P.do("vector", lambda e, bt=bt: e.tensor_copy(
            yfm_v[:, :, bt * 128:(bt + 1) * 128],
            pt.rearrange("p (c k) -> p c k", c=4)[:, :, 0:128]),
            deps=[tts], inc=DV)
        pt_free = ydone

    QOFF = 0
    ppr_free = None
    k2_done = None
    v2_done = None
    q2_done = None
    psc_free = [None, None]
    pout_free = None
    eB_free = [None, None]
    sO_free = None
    pb_free = t_bb
    opad_done = []
    for ph in range(8):
        k2_ev = None
        for kg in range(16):
            mm = None
            for ci in range(4):
                mm = P.do("tensor", lambda e, ci=ci, kg=kg, ph=ph: e.matmul(
                    pproj[:, :], wks[:, ci * D + ph * 128:ci * D + ph * 128 + 128],
                    yfm_v[:, ci, kg * 512:(kg + 1) * 512], start=(ci == 0), stop=(ci == 3)),
                    deps=([ydone, t_w, ppr_free] if ci == 0 else []), inc=TE)
            k2_ev = P.do("vector", lambda e, kg=kg: e.tensor_copy(k2[:, kg * 512:(kg + 1) * 512],
                                                                  pproj[:, :]), deps=[mm, k2_done], inc=DV)
            ppr_free = k2_ev
        k2_done = k2_ev
        v2_ev = None
        for bt in range(NBT):
            mm = None
            for ci in range(4):
                mm = P.do("tensor", lambda e, ci=ci, bt=bt, ph=ph: e.matmul(
                    pproj[:, 0:64], yfm_v[:, ci, bt * 128:(bt + 1) * 128],
                    wvs[:, ci * 512 + ph * 64:ci * 512 + (ph + 1) * 64],
                    start=(ci == 0), stop=(ci == 3)),
                    deps=([ppr_free, v2_done if bt == 0 else None] if ci == 0 else []), inc=TE)
            v2_ev = P.do("vector", lambda e, bt=bt: e.tensor_copy(
                v2_v[:, bt, 0:66].rearrange("p (j k) -> p j k", j=2)[:, :, 0:32],
                pproj[:, 0:64].rearrange("p (j k) -> p j k", j=2)),
                deps=[mm, t_v1], inc=DV)
            ppr_free = v2_ev
        v2_done = v2_ev
        q2_ev = None
        for qg in range(4):
            mm = None
            for ci in range(4):
                mm = P.do("tensor", lambda e, ci=ci, qg=qg, ph=ph: e.matmul(
                    pproj[:, :], wqs[:, ci * D + ph * 128:ci * D + ph * 128 + 128],
                    yfm_v[:, ci, QOFF + qg * 512:QOFF + (qg + 1) * 512],
                    start=(ci == 0), stop=(ci == 3)),
                    deps=([ppr_free, q2_done if qg == 0 else None] if ci == 0 else []), inc=TE)
            q2_ev = P.do("vector", lambda e, qg=qg: e.tensor_copy(q2[:, qg * 512:(qg + 1) * 512],
                                                                  pproj[:, :]), deps=[mm], inc=DV)
            ppr_free = q2_ev
        q2_done = q2_ev
        for j in range(2):
            h = ph * 2 + j
            for qg in range(4):
                omm = None
                for kt in range(NBT):
                    pp = kt % 2
                    smm = P.do("tensor", lambda e, j=j, kt=kt, qg=qg, pp=pp: e.matmul(
                        psc[pp][:, :], k2[j * 64:(j + 1) * 64, kt * 128:(kt + 1) * 128],
                        q2[j * 64:(j + 1) * 64, qg * 512:(qg + 1) * 512],
                        start=True, stop=True),
                        deps=[k2_done, q2_done, psc_free[pp]], inc=TE)
                    ex = P.do("scalar", lambda e, pp=pp: e.activation(eB[pp][:], psc[pp][:, :], AF.Exp,
                                                                     scale=0.125),
                              deps=[smm, eB_free[pp]], inc=AC)
                    psc_free[pp] = ex
                    omm = P.do("tensor", lambda e, j=j, kt=kt, pp=pp: e.matmul(
                        pout[:, :], v2_v[:, kt, j * 33:(j + 1) * 33], eB[pp][:],
                        start=(kt == 0), stop=(kt == NBT - 1)),
                        deps=[ex] + ([pout_free] if kt == 0 else []), inc=TE)
                    eB_free[pp] = omm
                ev = P.do("vector", lambda e: e.tensor_copy(sO[:], pout[:, :]),
                          deps=[omm, sO_free], inc=DV)
                pout_free = ev
                rc = P.do("vector", lambda e: e.reciprocal(sN[:], sO[32:33, :]), deps=[ev], inc=DV)
                bc = P.do("tensor", lambda e: e.matmul(pb[0:32, :], ones_row[0:1, 0:32], sN[0:1, :],
                                                       start=True, stop=True),
                          deps=[rc, pb_free], inc=TE)
                cc = P.do("vector", lambda e, h=h, qg=qg: e.tensor_mul(
                    opad_v[(h % 2) * 64:(h % 2) * 64 + 32, h // 2, qg * 512:(qg + 1) * 512],
                    sO[0:32, :], pb[0:32, :]), deps=[bc, t_z], inc=DV)
                pb_free = cc
                sO_free = cc
                opad_done.append(cc)


    hd = opad_done[-1]
    st_toks = [None, None]
    for t in range(NT):
        par = t % 2
        lx = P.do("sync", lambda e, t=t, par=par: e.dma_start(out=xtl[par][:], in_=xv[:, t, :]),
                  deps=[xl_free[par]], inc=LD, k=16)
        ev2 = None
        for hh in range(2):
            mm = None
            for ci in range(8):
                mm = P.do("tensor", lambda e, ci=ci, hh=hh, t=t: e.matmul(
                    pproj[:, 0:512], opad_v[:, ci, t * 128:(t + 1) * 128],
                    wos[:, ci * D + hh * 512:ci * D + (hh + 1) * 512],
                    start=(ci == 0), stop=(ci == 7)),
                    deps=([hd, ppr_free] if ci == 0 else []), inc=TE)
            a1 = P.do("vector", lambda e, hh=hh, par=par: e.tensor_add(
                outt[par][:, hh * 512:(hh + 1) * 512], pproj[:, 0:512],
                bob[:, hh * 512:(hh + 1) * 512]), deps=[mm, t_bb], inc=DV)
            ev2 = P.do("vector", lambda e, hh=hh, par=par: e.tensor_add(
                outt[par][:, hh * 512:(hh + 1) * 512], outt[par][:, hh * 512:(hh + 1) * 512],
                xtl[par][:, hh * 512:(hh + 1) * 512]),
                deps=[a1, lx, st_toks[par]], inc=DV)
            ppr_free = ev2
        xl_free[par] = ev2
        st_toks[par] = P.do("gpsimd", lambda e, t=t, par=par: e.dma_start(
            out=xov[:, t, :], in_=outt[par][:]), deps=[ev2], inc=ST, k=16)

    P.wait("gpsimd", *P.dma_done("ST"))
    return P.finalize()


def prep_l2_attn(d, xflat):
    Wq = d["L2_attn_Wq"]; Wk = d["L2_attn_Wk"]; Wv = d["L2_attn_Wv"]; Wo = d["L2_attn_Wo"]
    wqp = np.zeros((512, 1024), np.float32)
    wqp.reshape(512, 16, 64)[:, :, :32] = Wq.T.reshape(512, 16, 32)
    wkp = np.zeros((512, 1024), np.float32)
    wkp.reshape(512, 16, 64)[:, :, :32] = Wk.T.reshape(512, 16, 32)
    wop = np.zeros((1024, 1024), np.float32)
    wop.reshape(16, 64, 1024)[:, :32] = np.ascontiguousarray(Wo.T).reshape(16, 32, 1024)
    com = {
        "wqp": wqp, "wkp": wkp,
        "wvt": np.ascontiguousarray(Wv.T),
        "wop": wop,
        "bo": np.ascontiguousarray(d["L2_attn_bo"][None, :]),
        "ident": np.eye(128, dtype=np.float32),
    }
    maps = []
    for c in range(NCORE):
        b, so = c // 4, (c % 4) * 2048
        xb = xflat[b * 8192:(b + 1) * 8192]
        m = dict(com)
        m["x"] = np.ascontiguousarray(xflat[c * TOK:(c + 1) * TOK])
        m["xf"] = np.ascontiguousarray(np.concatenate([xb[so:], xb[:so]], axis=0))
        maps.append(m)
    return maps


_L2_ATTN_NC = None


def launch_l2_attn(xflat, d):
    global _L2_ATTN_NC
    if _L2_ATTN_NC is None:
        _L2_ATTN_NC = build_l2_attn()
    in_maps = prep_l2_attn(d, xflat)
    res = run_bass_kernel_spmd(_L2_ATTN_NC, in_maps, core_ids=list(range(NCORE)))
    return np.concatenate([res.results[c]["xo"] for c in range(NCORE)], axis=0)



def build_l2_ffn(debug=False):
    P = PB()
    x = P.din("x", [TOK, D])
    w1t = P.din("w1t", [D, 4096])
    w3t = P.din("w3t", [D, 4096])
    w2t = P.din("w2t", [4096, D])
    gwt = P.din("gwt", [D, 8])
    nwr = P.din("nwr", [1, D])
    a1t = P.din("a1t", [8 * D, 16])
    a3t = P.din("a3t", [8 * D, 16])
    a2t = P.din("a2t", [8 * 4096, 16])
    b1t = P.din("b1t", [8 * 16, 4096])
    b3t = P.din("b3t", [8 * 16, 4096])
    b2t = P.din("b2t", [8 * 16, D])
    ident = P.din("ident", [128, 128])
    xo = P.dout("xo", [TOK, D])
    dbg = P.dout("dbg", [8, 128, 512]) if debug else None

    NG = 4
    GT = 4

    xtl = [P.sb("xtl0", [128, D]), P.sb("xtl1", [128, D])]
    idb = P.sb("idb", [128, 128], BF16)
    idf = P.sb("idf", [128, 128])
    gws = P.sb("gws", [128, 8 * 8])
    nwrow = P.sb("nwrow", [1, D])
    nwb = P.sb("nwb", [128, D])
    ones_row = P.sb("ones_row", [1, 128])
    epsb = P.sb("epsb", [128, 1])
    eps2 = P.sb("eps2", [128, 1])
    st1 = P.sb("st1", [128, 8])
    sY = P.sb("sY", [128, D])
    sTV = P.sb("sTV", [128, D])
    sBF = P.sb("sBF", [128, D], BF16)
    yf32 = P.sb("yf32", [128, D])
    tfm = P.sb("tfm", [128, 8 * 512], BF16)
    base1 = P.sb("base1", [128, 32 * 512], BF16)
    base3 = P.sb("base3", [128, 32 * 512], BF16)
    accact = P.sb("accact", [128, 32 * 512], BF16)
    acc2 = P.sb("acc2", [128, GT * D])
    wbsb = P.sb("wbsb", [128, 8 * 512], BF16)
    lg = P.sb("lg", [128, 8])
    eq8 = P.sb("eq8", [128, 8])
    msk8 = P.sb("msk8", [128, 8])
    ind8 = P.sb("ind8", [128, 8])
    w8 = P.sb("w8", [128, 8])
    we_all = P.sb("we_all", [128, GT * 8])
    w1sf = [P.sb("w1sf0", [128, 8 * 128], BF16), P.sb("w1sf1", [128, 8 * 128], BF16)]
    w3sf = [P.sb("w3sf0", [128, 8 * 128], BF16), P.sb("w3sf1", [128, 8 * 128], BF16)]
    w2sf = [P.sb("w2sf0", [128, 256], BF16), P.sb("w2sf1", [128, 256], BF16)]
    a1s = P.sb("a1s", [128, 8 * 16], BF16)
    a3s = P.sb("a3s", [128, 8 * 16], BF16)
    a2s = P.sb("a2s", [128, 32 * 16], BF16)
    b1s = P.sb("b1s", [16, 4096], BF16)
    b3s = P.sb("b3s", [16, 4096], BF16)
    b2s = P.sb("b2s", [16, D], BF16)
    r1T = P.sb("r1T", [16, 512], BF16)
    r3T = P.sb("r3T", [16, 512], BF16)
    r2T = P.sb("r2T", [16, 512], BF16)
    rsc = P.sb("rsc", [128, 16])
    rbf = P.sb("rbf", [128, 16], BF16)
    sS = P.sb("sS", [128, 512])
    ones128b = P.sb("ones128b", [128, 128], BF16)
    wbtm = P.sb("wbtm", [128, 128], BF16)
    sAct = P.sb("sAct", [128, 512], BF16)
    sOC = P.sb("sOC", [128, 512])
    outt = [P.sb("outt0", [128, D]), P.sb("outt1", [128, D])]

    pt = P.psum("pt", [128, 512], BF16)
    pt32 = P.psum("pt32", [128, 512])
    pB = P.psum("pB", [128, 512])
    ph1 = P.psum("ph1", [128, 512])
    ph3 = P.psum("ph3", [128, 512])
    pmulti = P.psum("pmulti", [128, 512])
    pout = [P.psum("pout0", [128, 512]), P.psum("pout1", [128, 512])]

    LD, WL, DV, AC, TE, ST = "LD", "WL", "DV", "AC", "TE", "ST"
    for s in (LD, WL, DV, AC, TE, ST):
        P.sem(s)

    xv = x.rearrange("(t p) d -> p t d", p=128)
    xov = xo.rearrange("(t p) d -> p t d", p=128)
    w1v = w1t.rearrange("(c p) f -> p c f", p=128)
    w3v = w3t.rearrange("(c p) f -> p c f", p=128)
    w2v = w2t.rearrange("(c p) n -> p c n", p=128)
    a1v = a1t.rearrange("(e c p) r -> e p c r", e=8, p=128)
    a3v = a3t.rearrange("(e c p) r -> e p c r", e=8, p=128)
    a2v = a2t.rearrange("(e c p) r -> e p c r", e=8, p=128)
    b1v = b1t.rearrange("(e p) f -> e p f", e=8)
    b3v = b3t.rearrange("(e p) f -> e p f", e=8)
    b2v = b2t.rearrange("(e p) n -> e p n", e=8)
    tfm_v = tfm[:].rearrange("p (c k) -> p c k", c=8)

    t_id = P.do("gpsimd", lambda e: e.dma_start(out=idb[:], in_=ident[:]), inc=LD, k=16)
    t_idf = P.do("sync", lambda e: e.dma_start(out=idf[:], in_=ident[:]), inc=LD, k=16)
    t_gw = P.do("sync", lambda e: e.dma_start(out=gws[:].rearrange("p (c n) -> p c n", c=8),
                                              in_=gwt.rearrange("(c p) n -> p c n", p=128)), inc=LD, k=16)
    t_nw = P.do("sync", lambda e: e.dma_start(out=nwrow[:], in_=nwr[:]), inc=LD, k=16)
    t_eps = P.do("vector", lambda e: e.memset(epsb[:], 1e-5), inc=DV)
    t_ep2 = P.do("vector", lambda e: e.memset(eps2[:], 1e-6), inc=DV)
    t_ones = P.do("vector", lambda e: e.memset(ones_row[:], 1.0), inc=DV)
    t_o128 = P.do("vector", lambda e: e.memset(ones128b[:], 1.0), inc=DV)
    tb = P.do("tensor", lambda e: e.matmul(pB[:, :], ones_row[0:1, :], nwrow[0:1, 0:512],
                                           start=True, stop=True), deps=[t_nw, t_ones], inc=TE)
    t_nb = P.do("vector", lambda e: e.tensor_copy(nwb[:, 0:512], pB[:, :]), deps=[tb], inc=DV)
    tb = P.do("tensor", lambda e: e.matmul(pB[:, :], ones_row[0:1, :], nwrow[0:1, 512:1024],
                                           start=True, stop=True), deps=[t_nb], inc=TE)
    t_nb = P.do("vector", lambda e: e.tensor_copy(nwb[:, 512:1024], pB[:, :]), deps=[tb], inc=DV)

    def emit_ln_stats(src_ap, deps, epstile, mean=True):
        if mean:
            m1 = P.do("vector", lambda e: e.reduce_sum(st1[:, 0:1], src_ap, AX.X), deps=deps, inc=DV)
        q1 = P.do("scalar", lambda e: e.activation(sBF[:], src_ap, AF.Square, accum_out=st1[:, 1:2]),
                  deps=deps + [sbf_free], inc=AC)
        if mean:
            m1 = P.do("vector", lambda e: e.tensor_scalar_mul(st1[:, 0:1], st1[:, 0:1], 1.0 / D), deps=[m1], inc=DV)
            m2 = P.do("vector", lambda e: e.tensor_mul(st1[:, 2:3], st1[:, 0:1], st1[:, 0:1]), deps=[m1], inc=DV)
            vv = P.do("vector", lambda e: e.scalar_tensor_tensor(st1[:, 3:4], st1[:, 1:2], 1.0 / D, st1[:, 2:3],
                                                                 ALU.mult, ALU.subtract), deps=[m2, q1], inc=DV)
        else:
            vv = P.do("vector", lambda e: e.tensor_scalar_mul(st1[:, 3:4], st1[:, 1:2], 1.0 / D),
                      deps=[q1], inc=DV)
        lv = P.do("scalar", lambda e, epstile=epstile: e.activation(st1[:, 3:4], st1[:, 3:4], AF.Ln,
                                                                    bias=epstile[:, 0:1]),
                  deps=[vv, t_eps, t_ep2], inc=AC)
        return P.do("scalar", lambda e: e.activation(st1[:, 4:5], st1[:, 3:4], AF.Exp, scale=-0.5),
                    deps=[lv], inc=AC)

    xl_free = [None, None]
    st_toks = [None, None]
    sbf_free = None
    pt_free = None
    pt32_free = None
    sy_free = None
    stv_free = None
    sbf_free = None
    pmulti_free = None
    pB_free = t_nb
    base_free = None
    acc_done = None
    wfree = {}

    for g in range(NG):
        gate_done = None
        tfm_done = None
        for tl in range(GT):
            t = g * GT + tl
            par = tl % 2
            lx = P.do("sync", lambda e, t=t, par=par: e.dma_start(out=xtl[par][:], in_=xv[:, t, :]),
                      deps=[xl_free[par]], inc=LD, k=16)
            rs = emit_ln_stats(xtl[par][:], [lx], epsb)
            ty = P.do("vector", lambda e, par=par: e.tensor_scalar(
                sY[:], xtl[par][:], st1[:, 0:1], st1[:, 4:5], ALU.subtract, ALU.mult),
                deps=[rs, sy_free], inc=DV)
            xl_free[par] = ty
            rs2 = emit_ln_stats(sY[:], [ty], eps2, mean=False)
            tv = P.do("vector", lambda e: e.tensor_scalar(sTV[:], sY[:], 0.0, st1[:, 4:5],
                                                          ALU.add, ALU.mult),
                      deps=[rs2, stv_free], inc=DV)
            sy_free = tv
            tv = P.do("vector", lambda e: e.tensor_mul(sTV[:], sTV[:], nwb[:]), deps=[tv, t_nb], inc=DV)
            tc = P.do("vector", lambda e: e.tensor_copy(sBF[:], sTV[:]), deps=[tv], inc=DV)
            tts = None
            for c in range(8):
                tts = P.do("tensor", lambda e, c=c: e.transpose(
                    pt[:, (c % 4) * 128:(c % 4) * 128 + 128] if False else pt[:, (c % 4) * 128:(c % 4) * 128 + 128],
                    sBF[:, c * 128:(c + 1) * 128], idb[:]),
                    deps=[tc, t_id] + ([pt_free] if c == 0 else ([] if c != 4 else [pt_free])), inc=TE)
                if c == 3 or c == 7:
                    tev = P.do("vector", lambda e, c=c, tl=tl: e.tensor_copy(
                        tfm_v[:, c - 3:c + 1, tl * 128:(tl + 1) * 128],
                        pt.rearrange("p (cc k) -> p cc k", cc=4)[:, :, :]),
                        deps=[tts], inc=DV)
                    pt_free = tev
            tfm_done = tev
            sbf_free = tts
            tt32 = None
            gm = None
            for c in range(8):
                tt32 = P.do("tensor", lambda e, c=c: e.transpose(
                    pt32[:, (c % 4) * 128:(c % 4) * 128 + 128], sTV[:, c * 128:(c + 1) * 128], idf[:]),
                    deps=[tc, t_idf] + ([pt32_free] if (c % 4) == 0 else []), inc=TE)
                if c == 3 or c == 7:
                    tev32 = P.do("vector", lambda e, c=c: e.tensor_copy(
                        yf32[:, (c - 3) * 128:(c + 1) * 128], pt32[:, :]), deps=[tt32], inc=DV)
                    pt32_free = tev32
            stv_free = tt32
            for c in range(8):
                gm = P.do("tensor", lambda e, c=c: e.matmul(
                    pmulti[:, 0:8], yf32[:, c * 128:(c + 1) * 128],
                    gws[:, c * 8:(c + 1) * 8], start=(c == 0), stop=(c == 7)),
                    deps=([tev32, t_gw, pmulti_free] if c == 0 else []), inc=TE)
            gv = P.do("vector", lambda e: e.tensor_copy(lg[:], pmulti[:, 0:8]), deps=[gm], inc=DV)
            pmulti_free = gv
            a = P.do("vector", lambda e: e.reduce_max(st1[:, 5:6], lg[:], AX.X), deps=[gv], inc=DV)
            a = P.do("vector", lambda e: e.tensor_scalar(eq8[:], lg[:], st1[:, 5:6], None, ALU.is_ge),
                     deps=[a], inc=DV)
            a = P.do("vector", lambda e: e.scalar_tensor_tensor(msk8[:], eq8[:], -1e30, lg[:],
                                                                ALU.mult, ALU.add), deps=[a], inc=DV)
            a = P.do("vector", lambda e: e.reduce_max(st1[:, 6:7], msk8[:], AX.X), deps=[a], inc=DV)
            ti = P.do("vector", lambda e: e.tensor_scalar(ind8[:], lg[:], st1[:, 6:7], None, ALU.is_ge),
                      deps=[a], inc=DV)
            nm = P.do("vector", lambda e: e.tensor_scalar_mul(st1[:, 7:8], st1[:, 5:6], -1.0),
                      deps=[ti], inc=DV)
            ex = P.do("scalar", lambda e: e.activation(msk8[:], lg[:], AF.Exp, bias=st1[:, 7:8]),
                      deps=[nm], inc=AC)
            a = P.do("vector", lambda e: e.tensor_mul(w8[:], msk8[:], ind8[:]), deps=[ex], inc=DV)
            a = P.do("vector", lambda e: e.reduce_sum(st1[:, 5:6], w8[:], AX.X), deps=[a], inc=DV)
            a = P.do("vector", lambda e: e.reciprocal(st1[:, 6:7], st1[:, 5:6]), deps=[a], inc=DV)
            gate_done = P.do("vector", lambda e, tl=tl: e.tensor_scalar_mul(
                we_all[:, tl * 8:tl * 8 + 8], w8[:], st1[:, 6:7]), deps=[a], inc=DV)
        wb_done = None
        wbtm_free = None
        for tl in range(GT):
            for ei in range(8):
                a = P.do("vector", lambda e, ei=ei, tl=tl: e.tensor_scalar_mul(
                    wbtm[:], ones128b[:], we_all[:, tl * 8 + ei:tl * 8 + ei + 1]),
                    deps=[gate_done, t_o128, wbtm_free], inc=DV)
                ttw = P.do("tensor", lambda e: e.transpose(pt[:, 0:128], wbtm[:], idb[:]),
                           deps=[a, pt_free], inc=TE)
                wbtm_free = ttw
                wb_done = P.do("vector", lambda e, ei=ei, tl=tl: e.tensor_copy(
                    wbsb[:, ei * 512 + tl * 128:ei * 512 + (tl + 1) * 128], pt[:, 0:128]),
                    deps=[ttw], inc=DV)
                pt_free = wb_done

        if debug and g == 0:
            P.do("gpsimd", lambda e: e.dma_start(out=dbg[0], in_=sTV[:, 0:512]), deps=[gate_done], inc=ST, k=16)
            P.do("gpsimd", lambda e: e.dma_start(out=dbg[1], in_=wbsb[:, 0:512]), deps=[wb_done], inc=ST, k=16)
            P.do("gpsimd", lambda e: e.dma_start(out=dbg[7][:, 0:32], in_=we_all[:]), deps=[wb_done], inc=ST, k=16)
        base_done = None
        for f in range(32):
            b = f % 2
            wl1 = P.do("gpsimd", lambda e, f=f, b=b: e.dma_start(
                out=w1sf[b][:].rearrange("p (c k) -> p c k", c=8),
                in_=w1v[:, :, f * 128:(f + 1) * 128]), deps=[wfree.get(("w1", b))], inc=WL, k=16)
            wl3 = P.do("gpsimd", lambda e, f=f, b=b: e.dma_start(
                out=w3sf[b][:].rearrange("p (c k) -> p c k", c=8),
                in_=w3v[:, :, f * 128:(f + 1) * 128]), deps=[wfree.get(("w3", b))], inc=WL, k=16)
            mm1 = None
            for ci in range(8):
                mm1 = P.do("tensor", lambda e, ci=ci, b=b: e.matmul(
                    pB[:, :], w1sf[b][:, ci * 128:(ci + 1) * 128], tfm_v[:, ci, :],
                    start=(ci == 0), stop=(ci == 7)),
                    deps=([wl1, tfm_done, pB_free] if ci == 0 else []), inc=TE)
            ev1 = P.do("scalar", lambda e, f=f: e.copy(base1[:, f * 512:(f + 1) * 512], pB[:, :]),
                       deps=[mm1, base_free], inc=AC)
            pB_free = ev1
            mm3 = None
            for ci in range(8):
                mm3 = P.do("tensor", lambda e, ci=ci, b=b: e.matmul(
                    pB[:, :], w3sf[b][:, ci * 128:(ci + 1) * 128], tfm_v[:, ci, :],
                    start=(ci == 0), stop=(ci == 7)),
                    deps=([wl3, pB_free] if ci == 0 else []), inc=TE)
            wfree[("w1", b)] = mm1
            wfree[("w3", b)] = mm3
            base_done = P.do("scalar", lambda e, f=f: e.copy(base3[:, f * 512:(f + 1) * 512], pB[:, :]),
                             deps=[mm3], inc=AC)
            pB_free = base_done

        if debug and g == 0:
            P.do("gpsimd", lambda e: e.dma_start(out=dbg[2], in_=base1[:, 0:512]), deps=[base_done], inc=ST, k=16)
            P.do("gpsimd", lambda e: e.dma_start(out=dbg[3], in_=base3[:, 0:512]), deps=[base_done], inc=ST, k=16)
        r1_free = None
        r2_free = None
        sS_free = None
        sAct_free = None
        accact_done = None
        for ei in range(8):
            ba = ei % 2
            tl_a1 = P.do("gpsimd", lambda e, ei=ei: e.dma_start(
                out=a1s[:].rearrange("p (c r) -> p c r", c=8), in_=a1v[ei]),
                deps=[wfree.get("a1")], inc=WL, k=16)
            tl_a3 = P.do("gpsimd", lambda e, ei=ei: e.dma_start(
                out=a3s[:].rearrange("p (c r) -> p c r", c=8), in_=a3v[ei]),
                deps=[wfree.get("a3")], inc=WL, k=16)
            tl_a2 = P.do("gpsimd", lambda e, ei=ei: e.dma_start(
                out=a2s[:].rearrange("p (c r) -> p c r", c=32), in_=a2v[ei]),
                deps=[wfree.get("a2")], inc=WL, k=16)
            tl_b1 = P.do("gpsimd", lambda e, ei=ei: e.dma_start(out=b1s[:], in_=b1v[ei]),
                         deps=[wfree.get("b1")], inc=WL, k=16)
            tl_b3 = P.do("gpsimd", lambda e, ei=ei: e.dma_start(out=b3s[:], in_=b3v[ei]),
                         deps=[wfree.get("b3")], inc=WL, k=16)
            tl_b2 = P.do("gpsimd", lambda e, ei=ei: e.dma_start(out=b2s[:], in_=b2v[ei]),
                         deps=[wfree.get("b2")], inc=WL, k=16)
            for (asrc, rdst, tok_a) in ((a1s, r1T, tl_a1), (a3s, r3T, tl_a3)):
                rt_done = None
                for tl in range(GT):
                    mm = None
                    for ci in range(8):
                        mm = P.do("tensor", lambda e, ci=ci, tl=tl, asrc=asrc: e.matmul(
                            pmulti[:, 0:16], tfm_v[:, ci, tl * 128:(tl + 1) * 128],
                            asrc[:, ci * 16:(ci + 1) * 16], start=(ci == 0), stop=(ci == 7)),
                            deps=([tok_a, pmulti_free] if ci == 0 else []), inc=TE)
                    ev = P.do("vector", lambda e: e.tensor_scalar_mul(rsc[:], pmulti[:, 0:16], 2.0),
                              deps=[mm, r1_free], inc=DV)
                    pmulti_free = ev
                    cv = P.do("vector", lambda e: e.tensor_copy(rbf[:], rsc[:]), deps=[ev], inc=DV)
                    tt = P.do("tensor", lambda e: e.transpose(pt[0:16, 0:128], rbf[:], idb[:]),
                              deps=[cv, pt_free], inc=TE)
                    r1_free = tt
                    rt_done = P.do("vector", lambda e, tl=tl, rdst=rdst: e.tensor_copy(
                        rdst[:, tl * 128:(tl + 1) * 128], pt[0:16, 0:128]), deps=[tt], inc=DV)
                    pt_free = rt_done
            r2mm = None
            for f in range(32):
                m1 = P.do("tensor", lambda e, f=f: e.matmul(
                    ph1[:, :], idb[:], base1[:, f * 512:(f + 1) * 512], start=True, stop=False),
                    deps=[rt_done, base_done] + ([sS_free] if sS_free else []), inc=TE)
                m1 = P.do("tensor", lambda e, f=f: e.matmul(
                    ph1[:, :], b1s[0:16, f * 128:(f + 1) * 128], r1T[:, :], start=False, stop=True),
                    deps=[tl_b1], inc=TE)
                sl = P.do("scalar", lambda e: e.activation(sS[:], ph1[:, :], AF.Silu),
                          deps=[m1, sS_free], inc=AC)
                m3 = P.do("tensor", lambda e, f=f: e.matmul(
                    ph3[:, :], idb[:], base3[:, f * 512:(f + 1) * 512], start=True, stop=False),
                    deps=[sAct_free] if sAct_free else [], inc=TE)
                m3 = P.do("tensor", lambda e, f=f: e.matmul(
                    ph3[:, :], b3s[0:16, f * 128:(f + 1) * 128], r3T[:, :], start=False, stop=True),
                    deps=[tl_b3], inc=TE)
                av = P.do("vector", lambda e: e.tensor_tensor(sAct[:], sS[:], ph3[:, :], ALU.mult),
                          deps=[sl, m3, sAct_free], inc=DV)
                sS_free = av
                r2mm = P.do("tensor", lambda e, f=f: e.matmul(
                    pmulti[0:16, :], a2s[:, f * 16:(f + 1) * 16], sAct[:, :],
                    start=(f == 0), stop=(f == 31)),
                    deps=[av, tl_a2] + ([pmulti_free] if f == 0 else []), inc=TE)
                sAct_free = r2mm
                if ei == 0:
                    aw = P.do("vector", lambda e, f=f, ei=ei: e.tensor_tensor(
                        accact[:, f * 512:(f + 1) * 512], sAct[:, :],
                        wbsb[:, ei * 512:(ei + 1) * 512], ALU.mult),
                        deps=[av, wb_done, accact_done], inc=DV)
                else:
                    aw = P.do("vector", lambda e, f=f, ei=ei: e.tensor_tensor(
                        sOC[:, :], sAct[:, :], wbsb[:, ei * 512:(ei + 1) * 512], ALU.mult),
                        deps=[av, wb_done], inc=DV)
                    aw = P.do("vector", lambda e, f=f: e.tensor_add(
                        accact[:, f * 512:(f + 1) * 512], accact[:, f * 512:(f + 1) * 512],
                        sOC[:, :]), deps=[aw, accact_done], inc=DV)
                accact_done = aw
            if debug and g == 0 and ei == 0:
                P.do("gpsimd", lambda e: e.dma_start(out=dbg[4], in_=accact[:, 0:512]), deps=[accact_done], inc=ST, k=16)
                P.do("gpsimd", lambda e: e.dma_start(out=dbg[5][0:16, :], in_=r1T[:]), deps=[rt_done], inc=ST, k=16)
            ev = P.do("vector", lambda e: e.tensor_scalar_mul(r2T[:, :], pmulti[0:16, :], 2.0),
                      deps=[r2mm, r2_free], inc=DV)
            pmulti_free = ev
            for tl in range(GT):
                mm0 = P.do("tensor", lambda e, tl=tl: e.matmul(
                    pout[0][:, :], r2T[:, tl * 128:(tl + 1) * 128], b2s[0:16, 0:512],
                    start=True, stop=True), deps=[ev, tl_b2, wfree.get(("po", 0))], inc=TE)
                mm1_ = P.do("tensor", lambda e, tl=tl: e.matmul(
                    pout[1][:, :], r2T[:, tl * 128:(tl + 1) * 128], b2s[0:16, 512:1024],
                    start=True, stop=True), deps=[wfree.get(("po", 1))], inc=TE)
                if ei == 0:
                    e0 = P.do("vector", lambda e, tl=tl, ei=ei: e.tensor_scalar_mul(
                        acc2[:, tl * D:tl * D + 512], pout[0][:, :], we_all[:, tl * 8 + ei:tl * 8 + ei + 1]),
                        deps=[mm0, acc_done], inc=DV)
                    e1 = P.do("vector", lambda e, tl=tl, ei=ei: e.tensor_scalar_mul(
                        acc2[:, tl * D + 512:(tl + 1) * D], pout[1][:, :], we_all[:, tl * 8 + ei:tl * 8 + ei + 1]),
                        deps=[mm1_], inc=DV)
                else:
                    e0 = P.do("vector", lambda e, tl=tl, ei=ei: e.scalar_tensor_tensor(
                        acc2[:, tl * D:tl * D + 512], pout[0][:, :], we_all[:, tl * 8 + ei:tl * 8 + ei + 1],
                        acc2[:, tl * D:tl * D + 512], ALU.mult, ALU.add), deps=[mm0], inc=DV)
                    e1 = P.do("vector", lambda e, tl=tl, ei=ei: e.scalar_tensor_tensor(
                        acc2[:, tl * D + 512:(tl + 1) * D], pout[1][:, :], we_all[:, tl * 8 + ei:tl * 8 + ei + 1],
                        acc2[:, tl * D + 512:(tl + 1) * D], ALU.mult, ALU.add), deps=[mm1_], inc=DV)
                wfree[("po", 0)] = e0
                wfree[("po", 1)] = e1
                r2_free = e1
            wfree["a1"] = rt_done
            wfree["a3"] = rt_done
            wfree["a2"] = r2mm
            wfree["b1"] = m1
            wfree["b3"] = m3
            wfree["b2"] = e1
            acc_done = e1

        if debug and g == 0:
            P.do("gpsimd", lambda e: e.dma_start(out=dbg[6], in_=acc2[:, 0:512]), deps=[acc_done], inc=ST, k=16)
        for cp in range(4):
            for f in range(32):
                b = f % 2
                wl2 = P.do("gpsimd", lambda e, f=f, b=b, cp=cp: e.dma_start(
                    out=w2sf[b][:], in_=w2v[:, f, cp * 256:(cp + 1) * 256]),
                    deps=[wfree.get(("w2", b))], inc=WL, k=16)
                for j in range(2):
                    mm = P.do("tensor", lambda e, f=f, b=b, j=j: e.matmul(
                        pout[j][:, :], w2sf[b][:, j * 128:(j + 1) * 128], accact[:, f * 512:(f + 1) * 512],
                        start=(f == 0), stop=(f == 31)),
                        deps=([wl2] if j == 0 else []) + ([accact_done, wfree.get(("po", j))] if f == 0 else []), inc=TE)
                    if j == 1:
                        wfree[("w2", b)] = mm
                    if f == 31:
                        wfree[("po", j)] = None
            for j in range(2):
                cc = cp * 2 + j
                ev = P.do("scalar", lambda e, j=j: e.copy(sOC[:, :], pout[j][:, :]),
                          deps=[("TE", P._sems["TE"][1])], inc=AC)
                wfree[("po", j)] = ev
                for tl in range(GT):
                    tt = P.do("tensor", lambda e, tl=tl: e.transpose(
                        pt32[:, 0:128], sOC[:, tl * 128:(tl + 1) * 128], idf[:]),
                        deps=[ev, pt32_free], inc=TE)
                    ad = P.do("vector", lambda e, tl=tl, cc=cc: e.tensor_add(
                        acc2[:, tl * D + cc * 128:tl * D + (cc + 1) * 128],
                        acc2[:, tl * D + cc * 128:tl * D + (cc + 1) * 128], pt32[:, 0:128]),
                        deps=[tt], inc=DV)
                    pt32_free = ad
                    acc_done = ad
        if debug and g == 0:
            P.do("gpsimd", lambda e: e.dma_start(out=dbg[4], in_=accact[:, 0:512]), deps=[acc_done], inc=ST, k=16)
            P.do("gpsimd", lambda e: e.dma_start(out=dbg[5], in_=acc2[:, 0:512]), deps=[acc_done], inc=ST, k=16)
        for tl in range(GT):
            t = g * GT + tl
            par = tl % 2
            lx = P.do("sync", lambda e, t=t, par=par: e.dma_start(out=xtl[par][:], in_=xv[:, t, :]),
                      deps=[xl_free[par]], inc=LD, k=16)
            tf = P.do("vector", lambda e, tl=tl, par=par: e.tensor_add(
                outt[par][:], acc2[:, tl * D:(tl + 1) * D], xtl[par][:]),
                deps=[lx, acc_done, st_toks[par]], inc=DV)
            xl_free[par] = tf
            st_toks[par] = P.do("gpsimd", lambda e, t=t, par=par: e.dma_start(
                out=xov[:, t, :], in_=outt[par][:]), deps=[tf], inc=ST, k=16)
        acc_done = st_toks[1] if GT >= 2 else st_toks[0]

    P.wait("gpsimd", *P.dma_done("ST"))
    return P.finalize()


def prep_l2_ffn(d):
    w1 = d["L2_ffn_w1"]; w3 = d["L2_ffn_w3"]; w2 = d["L2_ffn_w2"]
    a1 = d["L2_ffn_a1"]; a3 = d["L2_ffn_a3"]; a2 = d["L2_ffn_a2"]
    b1 = d["L2_ffn_b1"]; b3 = d["L2_ffn_b3"]; b2 = d["L2_ffn_b2"]
    return {
        "w1t": np.ascontiguousarray(w1.T), "w3t": np.ascontiguousarray(w3.T),
        "w2t": np.ascontiguousarray(w2.T),
        "gwt": np.ascontiguousarray(d["L2_ffn_gate_w"].T),
        "nwr": np.ascontiguousarray(d["L2_ffn_norm_w"][None, :]),
        "a1t": np.concatenate([np.ascontiguousarray(a1[e].T) for e in range(8)], 0),
        "a3t": np.concatenate([np.ascontiguousarray(a3[e].T) for e in range(8)], 0),
        "a2t": np.concatenate([np.ascontiguousarray(a2[e].T) for e in range(8)], 0),
        "b1t": np.concatenate([np.ascontiguousarray(b1[e].T) for e in range(8)], 0),
        "b3t": np.concatenate([np.ascontiguousarray(b3[e].T) for e in range(8)], 0),
        "b2t": np.concatenate([np.ascontiguousarray(b2[e].T) for e in range(8)], 0),
        "ident": np.eye(128, dtype=np.float32),
    }


_L2_FFN_NC = None


def launch_l2_ffn(xflat, d, debug=False):
    global _L2_FFN_NC
    if _L2_FFN_NC is None or debug:
        _L2_FFN_NC = build_l2_ffn(debug)
    com = prep_l2_ffn(d)
    in_maps = []
    for c in range(NCORE):
        m = dict(com)
        m["x"] = np.ascontiguousarray(xflat[c * TOK:(c + 1) * TOK])
        in_maps.append(m)
    res = run_bass_kernel_spmd(_L2_FFN_NC, in_maps, core_ids=list(range(NCORE)))
    out = np.concatenate([res.results[c]["xo"] for c in range(NCORE)], axis=0)
    if debug:
        return out, res.results[0].get("dbg")
    return out



def build_l3_attn():
    P = PB()
    NBT = 64
    x = P.din("x", [TOK, D])
    xf = P.din("xf", [8192, D])
    wqm = P.din("wq", [D, D])
    wkm = P.din("wk", [D, D])
    wvm = P.din("wv", [D, D])
    wom = P.din("wo", [D, D])
    bqc = P.din("bqc", [128, 8])
    bkc = P.din("bkc", [128, 8])
    bvr = P.din("bvr", [1, D])
    qnsc = P.din("qnsc", [128, 4])
    rows = P.din("rows", [2, D])
    ib = P.din("ib", [1, 1])
    ones2m = P.din("ones2m", [128, 2])
    e2m = P.din("e2m", [2, 128])
    ident = P.din("ident", [128, 128])
    xo = P.dout("xo", [TOK, D])

    ksc = P.nc.dram_tensor("ksc", [8, 128, 8192], BF16, kind="Internal").ap()
    vsc = P.nc.dram_tensor("vsc", [NBT, 128, 16 * 65], BF16, kind="Internal").ap()

    xtl = [P.sb("xtl0", [128, D]), P.sb("xtl1", [128, D])]
    idb = P.sb("idb", [128, 128], BF16)
    wbig = P.sb("wbig", [128, 16 * D], BF16)
    wqs = wbig[:, 0:8 * D]
    wvs = wbig[:, 8 * D:16 * D]
    wks = P.sb("wks", [128, 8 * D], BF16)
    bqs = P.sb("bqs", [128, 8])
    bks = P.sb("bks", [128, 8])
    qns = P.sb("qns", [128, 4])
    rows_s = P.sb("rows_s", [1, 2 * D])
    bvrow = P.sb("bvrow", [1, D])
    bc = P.sb("bc", [128, 2 * D])
    bvb = P.sb("bvb", [128, D])
    ibs = P.sb("ibs", [1, 1])
    ibb = P.sb("ibb", [128, 1])
    ones2 = P.sb("ones2", [128, 2], BF16)
    e2 = P.sb("e2", [2, 128])
    ones_row = P.sb("ones_row", [1, 128])
    epsb = P.sb("epsb", [128, 1])
    eps6 = P.sb("eps6", [128, 1])
    st1 = P.sb("st1", [128, 8])
    smu = P.sb("smu", [2, 512])
    svar = P.sb("svar", [2, 512])
    st2t = P.sb("st2t", [2, 512])
    sY = P.sb("sY", [128, D])
    sBF = P.sb("sBF", [128, D], BF16)
    sK = P.sb("sK", [128, 512])
    sKb = P.sb("sKb", [128, 512], BF16)
    sKq = P.sb("sKq", [128, 512], BF16)
    sV = P.sb("sV", [128, 512])
    qfm = P.sb("qfm", [128, 8 * TOK], BF16)
    k2 = P.sb("k2", [128, 8192], BF16)
    v2 = P.sb("v2", [128, NBT * 130], BF16)
    eB = [P.sb("eB0", [128, 512], BF16), P.sb("eB1", [128, 512], BF16)]
    ofm = wbig
    outt = [P.sb("outt0", [128, D]), P.sb("outt1", [128, D])]

    pt = P.psum("pt", [128, 512], BF16)
    pproj = P.psum("pproj", [128, 512])
    pstA = P.psum("pstA", [2, 512])
    pstB = P.psum("pstB", [2, 512])
    pm1 = P.psum("pm1", [128, 512])
    pm2 = P.psum("pm2", [128, 512])
    pout = P.psum("pout", [64, 512])

    LD, DV, AC, TE, ST = "LD", "DV", "AC", "TE", "ST"
    for s in (LD, DV, AC, TE, ST):
        P.sem(s)

    xv = x.rearrange("(t p) d -> p t d", p=128)
    xfv = xf.rearrange("(t p) d -> p t d", p=128)
    xov = xo.rearrange("(t p) d -> p t d", p=128)
    qfm_v = qfm[:].rearrange("p (c k) -> p c k", c=8)
    ofm_v = ofm.rearrange("p (c k) -> p c k", c=8)
    v2_v = v2[:].rearrange("p (t j) -> p t j", t=NBT)

    t_wq = P.do("gpsimd", lambda e: e.dma_start(
        out=wqs.rearrange("p (c n) -> p c n", c=8),
        in_=wqm.rearrange("(c p) n -> p c n", p=128)), inc=LD, k=16)
    t_wk = P.do("gpsimd", lambda e: e.dma_start(
        out=wks[:].rearrange("p (c n) -> p c n", c=8),
        in_=wkm.rearrange("(c p) n -> p c n", p=128)), inc=LD, k=16)
    t_wv = P.do("gpsimd", lambda e: e.dma_start(
        out=wvs.rearrange("p (c n) -> p c n", c=8),
        in_=wvm.rearrange("(c p) n -> p c n", p=128)), inc=LD, k=16)
    t_id = P.do("gpsimd", lambda e: e.dma_start(out=idb[:], in_=ident[:]), inc=LD, k=16)
    t_o2 = P.do("gpsimd", lambda e: e.dma_start(out=ones2[:], in_=ones2m[:]), inc=LD, k=16)
    t_e2 = P.do("sync", lambda e: e.dma_start(out=e2[:], in_=e2m[:]), inc=LD, k=16)
    t_bq = P.do("sync", lambda e: e.dma_start(out=bqs[:], in_=bqc[:]), inc=LD, k=16)
    t_bk = P.do("sync", lambda e: e.dma_start(out=bks[:], in_=bkc[:]), inc=LD, k=16)
    t_bv = P.do("sync", lambda e: e.dma_start(out=bvrow[:], in_=bvr[:]), inc=LD, k=16)
    t_qn = P.do("sync", lambda e: e.dma_start(out=qns[:], in_=qnsc[:]), inc=LD, k=16)
    t_rw = P.do("sync", lambda e: e.dma_start(out=rows_s[:], in_=rows.rearrange("r d -> (r d)")[None, :]), inc=LD, k=16)
    t_ib = P.do("sync", lambda e: e.dma_start(out=ibs[:], in_=ib[:]), inc=LD, k=16)
    t_eps = P.do("vector", lambda e: e.memset(epsb[:], 1e-5), inc=DV)
    t_ep6 = P.do("vector", lambda e: e.memset(eps6[:], 1e-6), inc=DV)
    t_ones = P.do("vector", lambda e: e.memset(ones_row[:], 1.0), inc=DV)
    t_v1 = P.do("vector", lambda e: e.memset(v2[:], 1.0), inc=DV)

    pbf = None
    for i in range(3):
        srcrow = (lambda hh, i=i: rows_s[0:1, i * D + hh * 512:i * D + (hh + 1) * 512]) if i < 2 else \
                 (lambda hh: bvrow[0:1, hh * 512:(hh + 1) * 512])
        dstt = (lambda hh, i=i: bc[:, i * D + hh * 512:i * D + (hh + 1) * 512]) if i < 2 else \
               (lambda hh: bvb[:, hh * 512:(hh + 1) * 512])
        for hh in range(2):
            tb = P.do("tensor", lambda e, s=srcrow(hh): e.matmul(
                pm1[:, :], ones_row[0:1, :], s, start=True, stop=True),
                deps=[t_rw, t_bv, t_ones, pbf], inc=TE)
            pbf = P.do("vector", lambda e, dd=dstt(hh): e.tensor_copy(dd, pm1[:, :]), deps=[tb], inc=DV)
    tb = P.do("tensor", lambda e: e.matmul(pm1[:, 0:1], ones_row[0:1, :], ibs[0:1, :],
                                           start=True, stop=True), deps=[t_ib, pbf], inc=TE)
    t_ibb = P.do("vector", lambda e: e.tensor_copy(ibb[:], pm1[:, 0:1]), deps=[tb], inc=DV)
    pm1_free = t_ibb

    sbf_free = None

    def emit_ln_stats(src_ap, deps):
        m1 = P.do("vector", lambda e: e.reduce_sum(st1[:, 0:1], src_ap, AX.X), deps=deps, inc=DV)
        q1 = P.do("scalar", lambda e: e.activation(sBF[:], src_ap, AF.Square, accum_out=st1[:, 1:2]),
                  deps=deps + [sbf_free], inc=AC)
        m1 = P.do("vector", lambda e: e.tensor_scalar_mul(st1[:, 0:1], st1[:, 0:1], 1.0 / D), deps=[m1], inc=DV)
        m2 = P.do("vector", lambda e: e.tensor_mul(st1[:, 2:3], st1[:, 0:1], st1[:, 0:1]), deps=[m1], inc=DV)
        vv = P.do("vector", lambda e: e.scalar_tensor_tensor(st1[:, 3:4], st1[:, 1:2], 1.0 / D, st1[:, 2:3],
                                                             ALU.mult, ALU.subtract), deps=[m2, q1], inc=DV)
        lv = P.do("scalar", lambda e: e.activation(st1[:, 3:4], st1[:, 3:4], AF.Ln, bias=epsb[:, 0:1]),
                  deps=[vv, t_eps], inc=AC)
        return P.do("scalar", lambda e: e.activation(st1[:, 4:5], st1[:, 3:4], AF.Exp, scale=-0.5),
                    deps=[lv], inc=AC)

    xl_free = [None, None]
    pt_free = None
    sy_free = None
    ppr_free = None
    sk_free = None
    skb_free = None
    skq_free = None
    pstA_free = None
    pstB_free = None
    pm2_free = None
    sv_free = None
    y4 = [None]

    yg = P.sb("yg", [128, 8 * 512], BF16)
    yg_v = yg[:].rearrange("p (c k) -> p c k", c=8)

    def emit_y_tile(src_view, par, tokw):
        nonlocal pt_free, sy_free, sbf_free
        lx = P.do("sync", lambda e, par=par: e.dma_start(out=xtl[par][:], in_=src_view),
                  deps=[xl_free[par]], inc=LD, k=16)
        rs = emit_ln_stats(xtl[par][:], [lx])
        ty = P.do("vector", lambda e, par=par: e.tensor_scalar(
            sY[:], xtl[par][:], st1[:, 0:1], st1[:, 4:5], ALU.subtract, ALU.mult),
            deps=[rs, sy_free], inc=DV)
        xl_free[par] = ty
        tc = P.do("vector", lambda e: e.tensor_copy(sBF[:], sY[:]), deps=[ty], inc=DV)
        sy_free = tc
        tts = None
        tev = None
        for c in range(8):
            tts = P.do("tensor", lambda e, c=c: e.transpose(
                pt[:, (c % 4) * 128:(c % 4) * 128 + 128], sBF[:, c * 128:(c + 1) * 128], idb[:]),
                deps=[tc, t_id] + ([pt_free] if c % 4 == 0 else []), inc=TE)
            if c == 3 or c == 7:
                tev = P.do("vector", lambda e, c=c, tokw=tokw: e.tensor_copy(
                    yg_v[:, c - 3:c + 1, tokw * 128:(tokw + 1) * 128], pt.rearrange(
                        "p (cc k) -> p cc k", cc=4)[:, :, :]), deps=[tts], inc=DV)
                pt_free = tev
        sbf_free = tts
        return tev

    def emit_headln_chunk(cc, wsb, wtok, bias_sb, bcol, sc_col, bi_col, dst, dst_off, ydep):
        nonlocal ppr_free, sk_free, skb_free, skq_free, pstA_free, pstB_free, pm1_free, pm2_free
        mm = None
        for ci in range(8):
            mm = P.do("tensor", lambda e, ci=ci, cc=cc: e.matmul(
                pproj[:, :], wsb[:, ci * D + cc * 128:ci * D + (cc + 1) * 128],
                yg_v[:, ci, :], start=(ci == 0), stop=(ci == 7)),
                deps=([ydep, wtok, ppr_free] if ci == 0 else []), inc=TE)
        a = P.do("vector", lambda e, cc=cc: e.tensor_scalar(
            sK[:], pproj[:, :], bias_sb[:, bcol:bcol + 1], None, ALU.add),
            deps=[mm, sk_free, t_bq, t_bk], inc=DV)
        ppr_free = a
        cb = P.do("vector", lambda e: e.tensor_copy(sKb[:], sK[:]), deps=[a, skb_free], inc=DV)
        sq = P.do("scalar", lambda e: e.activation(sKq[:], sK[:], AF.Square), deps=[a, skq_free], inc=AC)
        s1 = P.do("tensor", lambda e: e.matmul(pstA[:, :], ones2[:, :], sKb[:, :],
                                               start=True, stop=True), deps=[cb, t_o2, pstA_free], inc=TE)
        s2 = P.do("tensor", lambda e: e.matmul(pstB[:, :], ones2[:, :], sKq[:, :],
                                               start=True, stop=True), deps=[sq, pstB_free], inc=TE)
        m1 = P.do("vector", lambda e: e.tensor_scalar_mul(smu[:], pstA[:, :], 1.0 / 64), deps=[s1], inc=DV)
        pstA_free = m1
        m2 = P.do("vector", lambda e: e.tensor_mul(st2t[:], smu[:], smu[:]), deps=[m1], inc=DV)
        vv = P.do("vector", lambda e: e.scalar_tensor_tensor(svar[:], pstB[:, :], 1.0 / 64, st2t[:],
                                                             ALU.mult, ALU.subtract), deps=[m2, s2], inc=DV)
        pstB_free = vv
        lv = P.do("scalar", lambda e: e.activation(svar[:], svar[:], AF.Ln, bias=eps6[0:2, 0:1]),
                  deps=[vv, t_ep6], inc=AC)
        rs = P.do("scalar", lambda e: e.activation(svar[:], svar[:], AF.Exp, scale=-0.5),
                  deps=[lv], inc=AC)
        b1 = P.do("tensor", lambda e: e.matmul(pm1[:, :], e2[:, :], smu[:, :], start=True, stop=True),
                  deps=[lv, t_e2, pm1_free], inc=TE)
        b2 = P.do("tensor", lambda e: e.matmul(pm2[:, :], e2[:, :], svar[:, :], start=True, stop=True),
                  deps=[rs, pm2_free], inc=TE)
        a = P.do("vector", lambda e: e.tensor_tensor(sK[:], sK[:], pm1[:, :], ALU.subtract),
                 deps=[b1], inc=DV)
        pm1_free = a
        a = P.do("vector", lambda e: e.tensor_tensor(sK[:], sK[:], pm2[:, :], ALU.mult),
                 deps=[a, b2], inc=DV)
        pm2_free = a
        out = P.do("vector", lambda e, dst=dst, dst_off=dst_off: e.tensor_scalar(
            dst[:, dst_off:dst_off + 512], sK[:], qns[:, sc_col:sc_col + 1],
            qns[:, bi_col:bi_col + 1], ALU.mult, ALU.add), deps=[a, t_qn], inc=DV)
        sk_free = out
        skb_free = s1
        skq_free = s2
        return out

    kst = None
    vst = None
    for bg in range(16):
        ydep = None
        for tl in range(4):
            ydep = emit_y_tile(xfv[:, bg * 4 + tl, :], tl % 2, tl)
        for cc in range(8):
            kd = emit_headln_chunk(cc, wks, t_wk, bks, cc, 2, 3, k2, 0, ydep)
            kst = P.do("gpsimd", lambda e, cc=cc, bg=bg: e.dma_start(
                out=ksc[cc][:, bg * 512:(bg + 1) * 512], in_=k2[:, 0:512]),
                deps=[kd], inc=ST, k=16)
            sk_free = kst
        if bg < 4:
            for cc in range(8):
                emit_headln_chunk(cc, wqs, t_wq, bqs, cc, 0, 1, qfm,
                                  cc * TOK + bg * 512, ydep)
        for tl in range(4):
            bt = bg * 4 + tl
            vev = None
            for hh in range(2):
                mm = None
                for ci in range(8):
                    mm = P.do("tensor", lambda e, ci=ci, hh=hh, tl=tl: e.matmul(
                        pproj[:, :], yg_v[:, ci, tl * 128:(tl + 1) * 128],
                        wvs[:, ci * D + hh * 512:ci * D + (hh + 1) * 512],
                        start=(ci == 0), stop=(ci == 7)),
                        deps=([ydep, t_wv, ppr_free] if ci == 0 else []), inc=TE)
                av = P.do("vector", lambda e, hh=hh: e.tensor_add(
                    sV[:], pproj[:, :], bvb[:, hh * 512:(hh + 1) * 512]),
                    deps=[mm, pbf, sv_free], inc=DV)
                ppr_free = av
                vev = P.do("vector", lambda e, hh=hh: e.tensor_copy(
                    v2_v[:, 0, hh * 8 * 65:(hh * 8 + 8) * 65].rearrange("p (g k) -> p g k", g=8)[:, :, 0:64]
                    if False else
                    v2[:, hh * 8 * 65:hh * 8 * 65 + 8 * 65].rearrange("p (g k) -> p g k", g=8)[:, :, 0:64],
                    sV[:].rearrange("p (g k) -> p g k", g=8)), deps=[av, t_v1, vst], inc=DV)
                sv_free = vev
            vst = P.do("gpsimd", lambda e, bt=bt: e.dma_start(
                out=vsc[bt], in_=v2[:, 0:1040]), deps=[vev], inc=ST, k=16)

    t_wo = P.do("gpsimd", lambda e: e.dma_start(
        out=wks[:].rearrange("p (c n) -> p c n", c=8),
        in_=wom.rearrange("(c p) n -> p c n", p=128)), deps=[kst, ("TE", P._sems["TE"][1])], inc=LD, k=16)

    k2_tok = None
    v2_tok = None
    psc = [pm1, pm2]
    psc_free = [pm1_free, pm2_free]
    eB_free = [None, None]
    pout_free = None
    o_done = None
    for hp in range(8):
        k2_tok = P.do("sync", lambda e, hp=hp: e.dma_start(out=k2[:], in_=ksc[hp]),
                      deps=[kst, psc_free[0], psc_free[1], k2_tok], inc=LD, k=16)
        v2_tok = P.do("sync", lambda e, hp=hp: e.dma_start(
            out=v2[:].rearrange("p (t j) -> p t j", t=NBT),
            in_=vsc[:, :, hp * 130:(hp + 1) * 130].rearrange("t p j -> p t j")),
            deps=[vst, v2_tok] + eB_free, inc=LD, k=16)
        for j in range(2):
            for qg in range(4):
                omm = None
                for kt in range(NBT):
                    pp = kt % 2
                    smm = P.do("tensor", lambda e, j=j, kt=kt, qg=qg, pp=pp, hp=hp: e.matmul(
                        psc[pp][:, :], k2[j * 64:(j + 1) * 64, kt * 128:(kt + 1) * 128],
                        qfm_v[j * 64:(j + 1) * 64, hp, qg * 512:(qg + 1) * 512],
                        start=True, stop=True),
                        deps=[k2_tok, psc_free[pp]], inc=TE)
                    ex = P.do("scalar", lambda e, pp=pp: e.activation(eB[pp][:], psc[pp][:, :], AF.Sigmoid,
                                                                     bias=ibb[:, 0:1]),
                              deps=[smm, t_ibb, eB_free[pp]], inc=AC)
                    psc_free[pp] = ex
                    omm = P.do("tensor", lambda e, j=j, kt=kt, pp=pp: e.matmul(
                        pout[:, :], v2_v[:, kt, j * 65:j * 65 + 64], eB[pp][:],
                        start=(kt == 0), stop=(kt == NBT - 1)),
                        deps=[ex, v2_tok] + ([pout_free] if kt == 0 else []), inc=TE)
                    eB_free[pp] = omm
                o_done = P.do("vector", lambda e, hp=hp, j=j, qg=qg: e.tensor_copy(
                    ofm_v[j * 64:(j + 1) * 64, hp, qg * 512:(qg + 1) * 512], pout[:, :]),
                    deps=[omm, o_done], inc=DV)
                pout_free = o_done

    st_toks = [None, None]
    for t in range(NT):
        par = t % 2
        lx = P.do("sync", lambda e, t=t, par=par: e.dma_start(out=xtl[par][:], in_=xv[:, t, :]),
                  deps=[xl_free[par]], inc=LD, k=16)
        ev2 = None
        for hh in range(2):
            mm = None
            for ci in range(8):
                mm = P.do("tensor", lambda e, ci=ci, hh=hh, t=t: e.matmul(
                    pproj[:, :], ofm_v[:, ci, t * 128:(t + 1) * 128],
                    wks[:, ci * D + hh * 512:ci * D + (hh + 1) * 512],
                    start=(ci == 0), stop=(ci == 7)),
                    deps=([o_done, t_wo, ppr_free] if ci == 0 else []), inc=TE)
            a1 = P.do("vector", lambda e, hh=hh, par=par: e.tensor_add(
                outt[par][:, hh * 512:(hh + 1) * 512], pproj[:, :], bc[:, hh * 512:(hh + 1) * 512]),
                deps=[mm, st_toks[par]], inc=DV)
            ppr_free = a1
            a2 = P.do("vector", lambda e, hh=hh, par=par: e.tensor_mul(
                outt[par][:, hh * 512:(hh + 1) * 512], outt[par][:, hh * 512:(hh + 1) * 512],
                bc[:, D + hh * 512:D + (hh + 1) * 512]), deps=[a1], inc=DV)
            ev2 = P.do("vector", lambda e, hh=hh, par=par: e.tensor_add(
                outt[par][:, hh * 512:(hh + 1) * 512], outt[par][:, hh * 512:(hh + 1) * 512],
                xtl[par][:, hh * 512:(hh + 1) * 512]), deps=[a2, lx], inc=DV)
        xl_free[par] = ev2
        st_toks[par] = P.do("gpsimd", lambda e, t=t, par=par: e.dma_start(
            out=xov[:, t, :], in_=outt[par][:]), deps=[ev2], inc=ST, k=16)

    P.wait("gpsimd", *P.dma_done("ST"))
    return P.finalize()


def prep_l3_attn(d, xflat):
    qn_s = np.tile(d["L3_attn_qn_scale"], 2)
    qn_b = np.tile(d["L3_attn_qn_bias"], 2)
    kn_s = np.tile(d["L3_attn_kn_scale"], 2)
    kn_b = np.tile(d["L3_attn_kn_bias"], 2)
    ones2 = np.zeros((128, 2), np.float32)
    ones2[0:64, 0] = 1.0
    ones2[64:128, 1] = 1.0
    com = {
        "wq": d["L3_attn_Wq"], "wk": d["L3_attn_Wk"], "wv": d["L3_attn_Wv"],
        "wo": d["L3_attn_Wo"],
        "bqc": np.ascontiguousarray(d["L3_attn_bq"].reshape(8, 128).T),
        "bkc": np.ascontiguousarray(d["L3_attn_bk"].reshape(8, 128).T),
        "bvr": np.ascontiguousarray(d["L3_attn_bv"][None, :]),
        "qnsc": np.ascontiguousarray(np.stack([qn_s, qn_b, kn_s, kn_b], axis=1)),
        "rows": np.ascontiguousarray(np.stack([d["L3_attn_bo"], d["L3_attn_gamma"]])),
        "ib": np.ascontiguousarray(d["L3_attn_init_bias"].reshape(1, 1)),
        "ones2m": ones2,
        "e2m": np.ascontiguousarray(ones2.T),
        "ident": np.eye(128, dtype=np.float32),
    }
    maps = []
    for c in range(NCORE):
        b, so = c // 4, (c % 4) * 2048
        xb = xflat[b * 8192:(b + 1) * 8192]
        m = dict(com)
        m["x"] = np.ascontiguousarray(xflat[c * TOK:(c + 1) * TOK])
        m["xf"] = np.ascontiguousarray(np.concatenate([xb[so:], xb[:so]], axis=0))
        maps.append(m)
    return maps


_L3_ATTN_NC = None


def launch_l3_attn(xflat, d):
    global _L3_ATTN_NC
    if _L3_ATTN_NC is None:
        _L3_ATTN_NC = build_l3_attn()
    in_maps = prep_l3_attn(d, xflat)
    res = run_bass_kernel_spmd(_L3_ATTN_NC, in_maps, core_ids=list(range(NCORE)))
    return np.concatenate([res.results[c]["xo"] for c in range(NCORE)], axis=0)



def build_l3_ffn(debug=False):
    P = PB()
    EXT = 3072
    NET = EXT // 128
    x = P.din("x", [TOK, D])
    xe = P.din("xe", [EXT, D])
    wqkv = P.din("wqkv", [2 * D, 1536])
    wo = P.din("wo", [2 * 512, D])
    w1a = P.din("w1a", [2 * D, 2816])
    w1g = P.din("w1g", [2 * D, 2816])
    w2m = P.din("w2m", [2 * 2816, D])
    b1ac = P.din("b1ac", [2 * 128, 22])
    b1gc = P.din("b1gc", [2 * 128, 22])
    qksc = P.din("qksc", [128, 4])
    lnrows = P.din("lnrows", [6, D])
    cq = P.din("cq", [128, EXT])
    sq = P.din("sq", [128, EXT])
    ckt = P.din("ckt", [128, EXT])
    skt = P.din("skt", [128, EXT])
    msame = P.din("msame", [128, 128])
    mprev = P.din("mprev", [128, 128])
    mpfp = P.din("mpfp", [128, 128])
    mpff = P.din("mpff", [128, 128])
    rt = P.din("rt", [128, 128])
    ones2m = P.din("ones2m", [128, 2])
    e2m = P.din("e2m", [2, 128])
    ident = P.din("ident", [128, 128])
    xo = P.dout("xo", [TOK, D])
    dbg = P.dout("dbg", [6, 3072, D]) if debug else None

    xsc = P.nc.dram_tensor("xsc", [1 + EXT, D], F32, kind="Internal").ap()

    xtl = [P.sb("xtl0", [128, D]), P.sb("xtl1", [128, D])]
    idb = P.sb("idb", [128, 128], BF16)
    rtb = P.sb("rtb", [128, 128], BF16)
    ones2 = P.sb("ones2", [128, 2], BF16)
    e2 = P.sb("e2", [2, 128])
    epsb = P.sb("epsb", [128, 1])
    ep24 = P.sb("ep24", [128, 1])
    st1 = P.sb("st1", [128, 8])
    sY = P.sb("sY", [128, D])
    sBF = P.sb("sBF", [128, D], BF16)
    AR = 24 * 520
    big = P.sb("big", [128, 4 * 3072 + 4 * 3072 + AR], BF16)
    qfm = big[:, 0:12288]
    kfm = big[:, 12288:24576]
    vtm = big[:, 24576:24576 + AR]
    w2s = big[:, 0:22528]
    hact = big[:, 22528:22528 + 11264]
    qfm_v = qfm.rearrange("p (c k) -> p c k", c=4)
    kfm_v = kfm.rearrange("p (c k) -> p c k", c=4)
    vtm_v = vtm.rearrange("p (t j) -> p t j", t=NET)
    w2s_v = w2s.rearrange("p (c k) -> p c k", c=22)
    hact_v = hact.rearrange("p (c k) -> p c k", c=22)
    wqkvs = P.sb("wqkvs", [128, 8 * 1536], BF16)
    wos = P.sb("wos", [128, 4 * D], BF16)
    w1as = [P.sb("w1as0", [128, 8 * 128], BF16), P.sb("w1as1", [128, 8 * 128], BF16)]
    w1gs = [P.sb("w1gs0", [128, 8 * 128], BF16), P.sb("w1gs1", [128, 8 * 128], BF16)]
    b1as = P.sb("b1as", [128, 22])
    b1gs = P.sb("b1gs", [128, 22])
    qkss = P.sb("qkss", [128, 4])
    lnrs = P.sb("lnrs", [1, 3 * D])
    lngb = P.sb("lngb", [128, D])
    lnbb = P.sb("lnbb", [128, D])
    b2b = P.sb("b2b", [128, D])
    cqs = P.sb("cqs", [128, EXT], BF16)
    sqs = P.sb("sqs", [128, EXT], BF16)
    cks = P.sb("cks", [128, EXT], BF16)
    sks = P.sb("sks", [128, EXT], BF16)
    msa = P.sb("msa", [128, 128])
    mpr = P.sb("mpr", [128, 128])
    mpp = P.sb("mpp", [128, 128])
    mpf = P.sb("mpf", [128, 128])
    ones_row = P.sb("ones_row", [1, 128])
    hg = P.sb("hg", [128, 8 * 512], BF16)
    hg_v = hg[:].rearrange("p (c k) -> p c k", c=8)
    sK = P.sb("sK", [128, 512])
    sKb = P.sb("sKb", [128, 512], BF16)
    sKq = P.sb("sKq", [128, 512], BF16)
    smu = P.sb("smu", [2, 512])
    sMSK = P.sb("sMSK", [128, 128])
    eB = P.sb("eB", [128, 128], BF16)
    sOL = P.sb("sOL", [65, 128])
    sNL = P.sb("sNL", [1, 128])
    ofb = P.sb("ofb", [128, 4 * 512], BF16)
    ofb_v = ofb[:].rearrange("p (c k) -> p c k", c=4)
    sA = P.sb("sA", [128, 512])
    sG = P.sb("sG", [128, 512])
    t1 = sA[:]
    t2 = sG[:]
    zrow = lnrs[0:1, 0:D]

    pt = P.psum("pt", [128, 512], BF16)
    pproj = P.psum("pproj", [128, 512])
    pstB = P.psum("pstB", [2, 512])
    pm = P.psum("pm", [128, 512])
    pa = P.psum("pa", [128, 512])
    pg = P.psum("pg", [128, 512])
    pol = P.psum("pol", [128, 128])

    LD, DV, AC, TE, ST = "LD", "DV", "AC", "TE", "ST"
    for s in (LD, DV, AC, TE, ST):
        P.sem(s)

    xv = x.rearrange("(t p) d -> p t d", p=128)
    xev = xe.rearrange("(t p) d -> p t d", p=128)
    xov = xo.rearrange("(t p) d -> p t d", p=128)
    wqkv_vv = wqkv.rearrange("(l c p) n -> l p c n", l=2, p=128)
    wo_vv = wo.rearrange("(l c p) n -> l p c n", l=2, p=128)
    w1a_vv = w1a.rearrange("(l c p) n -> l p c n", l=2, p=128)
    w1g_vv = w1g.rearrange("(l c p) n -> l p c n", l=2, p=128)
    w2_vv = w2m.rearrange("(l c p) n -> l p c n", l=2, p=128)
    b1a_vv = b1ac.rearrange("(l p) n -> l p n", l=2)
    b1g_vv = b1gc.rearrange("(l p) n -> l p n", l=2)

    for (dst, srct) in ((idb, ident), (rtb, rt), (ones2, ones2m),
                        (cqs, cq), (sqs, sq), (cks, ckt), (sks, skt)):
        P.do("gpsimd", lambda e, dst=dst, srct=srct: e.dma_start(out=dst[:], in_=srct[:]), inc=LD, k=16)
    t_gload = P.do("gpsimd", lambda e: e.dma_start(out=e2[:], in_=e2m[:]), inc=LD, k=16)
    for (dst, srct) in ((msa, msame), (mpr, mprev), (mpp, mpfp), (mpf, mpff), (qkss, qksc)):
        P.do("sync", lambda e, dst=dst, srct=srct: e.dma_start(out=dst[:], in_=srct[:]), inc=LD, k=16)
    t_eps = P.do("vector", lambda e: e.memset(epsb[:], 1e-5), inc=DV)
    t_e24 = P.do("vector", lambda e: e.memset(ep24[:], 1e-24), inc=DV)
    t_ones = P.do("vector", lambda e: e.memset(ones_row[:], 1.0), inc=DV)
    t_v1 = P.do("vector", lambda e: e.memset(big[:], 1.0), inc=DV)
    t_zr = P.do("vector", lambda e: e.memset(zrow, 0.0), inc=DV)
    t_z0 = P.do("gpsimd", lambda e: e.dma_start(out=xsc[0:1, :], in_=zrow), deps=[t_zr], inc=ST, k=16)

    def emit_ln_stats(src_ap, deps):
        m1 = P.do("vector", lambda e: e.reduce_sum(st1[:, 0:1], src_ap, AX.X), deps=deps, inc=DV)
        q1 = P.do("scalar", lambda e: e.activation(sBF[:], src_ap, AF.Square, accum_out=st1[:, 1:2]),
                  deps=deps + [sbf_free], inc=AC)
        m1 = P.do("vector", lambda e: e.tensor_scalar_mul(st1[:, 0:1], st1[:, 0:1], 1.0 / D), deps=[m1], inc=DV)
        m2 = P.do("vector", lambda e: e.tensor_mul(st1[:, 2:3], st1[:, 0:1], st1[:, 0:1]), deps=[m1], inc=DV)
        vv = P.do("vector", lambda e: e.scalar_tensor_tensor(st1[:, 3:4], st1[:, 1:2], 1.0 / D, st1[:, 2:3],
                                                             ALU.mult, ALU.subtract), deps=[m2, q1], inc=DV)
        lv = P.do("scalar", lambda e: e.activation(st1[:, 3:4], st1[:, 3:4], AF.Ln, bias=epsb[:, 0:1]),
                  deps=[vv, t_eps], inc=AC)
        return P.do("scalar", lambda e: e.activation(st1[:, 4:5], st1[:, 3:4], AF.Exp, scale=-0.5),
                    deps=[lv], inc=AC)

    xl_free = [None, None]
    sbf_free = None
    sy_free = None
    pt_free = None
    xst = [None] * NET

    for t in range(NET):
        par = t % 2
        lx = P.do("sync", lambda e, t=t, par=par: e.dma_start(out=xtl[par][:], in_=xev[:, t, :]),
                  deps=[xl_free[par]], inc=LD, k=16)
        rs = emit_ln_stats(xtl[par][:], [lx])
        ty = P.do("vector", lambda e, par=par: e.tensor_scalar(
            sY[:], xtl[par][:], st1[:, 0:1], st1[:, 4:5], ALU.subtract, ALU.mult),
            deps=[rs, sy_free], inc=DV)
        xl_free[par] = ty
        xst[t] = P.do("gpsimd", lambda e, t=t: e.dma_start(out=xsc[1 + t * 128:1 + (t + 1) * 128, :],
                                                           in_=sY[:]), deps=[ty, t_z0], inc=ST, k=16)
        sy_free = xst[t]

    if debug:
        P.do("gpsimd", lambda e: e.dma_start(out=dbg[0], in_=xsc[1:1 + 3072, :]),
             deps=[xst[23]], inc=ST, k=16)
    pm_free = None

    lnr_used = None

    def emit_row_bcasts(l):
        nonlocal pm_free, lnr_used
        t_lnr = P.do("sync", lambda e, l=l: e.dma_start(
            out=lnrs[:], in_=lnrows[3 * l:3 * l + 3].rearrange("r d -> (r d)")[None, :]),
            deps=[t_z0, lnr_used], inc=LD, k=16)
        done = None
        for (i, dst) in ((0, lngb), (1, lnbb), (2, b2b)):
            for hh in range(2):
                tb = P.do("tensor", lambda e, i=i, hh=hh: e.matmul(
                    pm[:, :], ones_row[0:1, :], lnrs[0:1, i * D + hh * 512:i * D + (hh + 1) * 512],
                    start=True, stop=True), deps=[t_lnr, t_ones, pm_free], inc=TE)
                done = P.do("vector", lambda e, dst=dst, hh=hh: e.tensor_copy(
                    dst[:, hh * 512:(hh + 1) * 512], pm[:, :]), deps=[tb], inc=DV)
                pm_free = done
        lnr_used = done
        return done

    ppr_free = None
    sk_free = None
    skb_free = None
    skq_free = None
    pstB_free = None
    kq_done = None
    v_done = None
    msk_free = None
    eB_free = None
    pol_free = None
    sol_free = None
    o_done = None
    wf = {}

    for l in range(2):
        t_v1 = P.do("vector", lambda e: e.memset(vtm[:, :], 1.0),
                    deps=[wf.get("arena")], inc=DV)
        t_wqkv = P.do("gpsimd", lambda e, l=l: e.dma_start(
            out=wqkvs[:].rearrange("p (c n) -> p c n", c=8), in_=wqkv_vv[l]),
            deps=[wf.get("wqkv")], inc=LD, k=16)
        t_wo = P.do("gpsimd", lambda e, l=l: e.dma_start(
            out=wos[:].rearrange("p (c n) -> p c n", c=4), in_=wo_vv[l]),
            deps=[wf.get("wo")], inc=LD, k=16)
        for gq in range(6):
            ydep = None
            for tl in range(4):
                t = gq * 4 + tl
                par = tl % 2
                l1t = P.do("sync", lambda e, t=t, par=par: e.dma_start(
                    out=xtl[par][:, 0:512], in_=xsc[1 + t * 128:1 + (t + 1) * 128, 0:512]),
                    deps=[xl_free[par], xst[t]], inc=LD, k=16)
                l2t = P.do("sync", lambda e, t=t, par=par: e.dma_start(
                    out=xtl[par][:, 512:1024], in_=xsc[t * 128:(t + 1) * 128, 512:1024]),
                    deps=[xst[t - 1] if t > 0 else t_z0, xst[t]], inc=LD, k=16)
                tc = P.do("vector", lambda e, par=par: e.tensor_copy(sBF[:], xtl[par][:]),
                          deps=[l1t, l2t, sbf_free], inc=DV)
                xl_free[par] = tc
                tts = None
                for c in range(8):
                    tts = P.do("tensor", lambda e, c=c: e.transpose(
                        pt[:, (c % 4) * 128:(c % 4) * 128 + 128], sBF[:, c * 128:(c + 1) * 128], idb[:]),
                        deps=[tc] + ([pt_free] if c % 4 == 0 else []), inc=TE)
                    if c == 3 or c == 7:
                        ydep = P.do("vector", lambda e, c=c, tl=tl: e.tensor_copy(
                            hg_v[:, c - 3:c + 1, tl * 128:(tl + 1) * 128],
                            pt.rearrange("p (cc k) -> p cc k", cc=4)[:, :, :]), deps=[tts], inc=DV)
                        pt_free = ydep
                sbf_free = tts
            for iscl, (c0, ctab, stab, scol) in enumerate(
                    (((0), cqs, sqs, 4 * 0 + 0), ((4), cks, sks, 0))):
                sccol = 2 * l + iscl
                for cc in range(4):
                    mm = None
                    for ci in range(8):
                        mm = P.do("tensor", lambda e, ci=ci, cc=cc, c0=c0: e.matmul(
                            pproj[:, :], wqkvs[:, ci * 1536 + (c0 + cc) * 128:ci * 1536 + (c0 + cc + 1) * 128],
                            hg_v[:, ci, :], start=(ci == 0), stop=(ci == 7)),
                            deps=([ydep, t_wqkv, ppr_free] if ci == 0 else []), inc=TE)
                    a = P.do("vector", lambda e: e.tensor_copy(sK[:], pproj[:, :]),
                             deps=[mm, sk_free], inc=DV)
                    ppr_free = a
                    sqq = P.do("scalar", lambda e: e.activation(sKq[:], sK[:], AF.Square),
                               deps=[a, skq_free], inc=AC)
                    s2 = P.do("tensor", lambda e: e.matmul(pstB[:, :], ones2[:, :], sKq[:, :],
                                                           start=True, stop=True),
                              deps=[sqq, pstB_free], inc=TE)
                    mx = P.do("vector", lambda e: e.tensor_scalar(smu[:], pstB[:, :], 1e-24, None,
                                                                  ALU.max), deps=[s2], inc=DV)
                    pstB_free = mx
                    lv = P.do("scalar", lambda e: e.activation(smu[:], smu[:], AF.Ln), deps=[mx], inc=AC)
                    rv = P.do("scalar", lambda e: e.activation(smu[:], smu[:], AF.Exp, scale=-0.5),
                              deps=[lv], inc=AC)
                    bz = P.do("tensor", lambda e: e.matmul(pm[:, :], e2[:, :], smu[:, :],
                                                           start=True, stop=True),
                              deps=[rv, t_gload, pm_free], inc=TE)
                    a = P.do("vector", lambda e: e.tensor_tensor(sK[:], sK[:], pm[:, :], ALU.mult),
                             deps=[bz], inc=DV)
                    pm_free = a
                    a = P.do("vector", lambda e, sccol=sccol: e.tensor_scalar(
                        sKb[:], sK[:], qkss[:, sccol:sccol + 1], None, ALU.mult),
                        deps=[a, skb_free], inc=DV)
                    sk_free = a
                    rmm = P.do("tensor", lambda e: e.matmul(pm[:, :], rtb[:], sKb[:, :],
                                                            start=True, stop=True),
                               deps=[a, pm_free], inc=TE)
                    a1 = P.do("vector", lambda e, ctab=ctab, gq=gq: e.tensor_tensor(
                        t1[:, :], sKb[:, :], ctab[:, gq * 512:(gq + 1) * 512], ALU.mult),
                        deps=[rmm], inc=DV)
                    skb_free = a1
                    a2 = P.do("vector", lambda e, stab=stab, gq=gq: e.tensor_tensor(
                        t2[:, :], pm[:, :], stab[:, gq * 512:(gq + 1) * 512], ALU.mult),
                        deps=[a1], inc=DV)
                    pm_free = a2
                    dstv = qfm_v if c0 == 0 else kfm_v
                    kq_done = P.do("vector", lambda e, dstv=dstv, cc=cc, gq=gq: e.tensor_add(
                        dstv[:, cc, gq * 512:(gq + 1) * 512], t1[:, :], t2[:, :]),
                        deps=[a2], inc=DV)
                    skq_free = s2
            for tl in range(4):
                t = gq * 4 + tl
                vev = None
                mm = None
                for ci in range(8):
                    mm = P.do("tensor", lambda e, ci=ci, tl=tl: e.matmul(
                        pproj[:, :], hg_v[:, ci, tl * 128:(tl + 1) * 128],
                        wqkvs[:, ci * 1536 + 1024:ci * 1536 + 1536],
                        start=(ci == 0), stop=(ci == 7)),
                        deps=([ydep, ppr_free] if ci == 0 else []), inc=TE)
                v_done = P.do("vector", lambda e, t=t: e.tensor_copy(
                    vtm_v[:, t, :].rearrange("p (g k) -> p g k", g=8)[:, :, 0:64],
                    pproj[:, :].rearrange("p (g k) -> p g k", g=8)), deps=[mm, t_v1, v_done], inc=DV)
                ppr_free = v_done

        for qb in range(6):
            ob_done = None
            for h in range(8):
                hb = (h % 2) * 64
                hc = h // 2
                for r in range(4):
                    qt = qb * 4 + r
                    kts = [kt for kt in range(qt - 4, qt + 1) if kt >= 0]
                    omm = None
                    for ki, kt in enumerate(kts):
                        first = (ki == 0)
                        last = (ki == len(kts) - 1)
                        dt = qt - kt
                        smm = P.do("tensor", lambda e, hb=hb, hc=hc, kt=kt, qt=qt: e.matmul(
                            pproj[:, 0:128], kfm_v[hb:hb + 64, hc, kt * 128:(kt + 1) * 128],
                            qfm_v[hb:hb + 64, hc, qt * 128:(qt + 1) * 128],
                            start=True, stop=True), deps=[kq_done, v_done, ppr_free], inc=TE)
                        own_b0 = (8 <= qt <= 11) and (kt < 8)
                        if dt == 0:
                            mtile = msa
                        elif dt == 4:
                            mtile = mpp if own_b0 else mpr
                        else:
                            mtile = mpf if own_b0 else None
                        if mtile is not None:
                            am = P.do("vector", lambda e, mtile=mtile: e.tensor_tensor(
                                sMSK[:], pproj[:, 0:128], mtile[:], ALU.add),
                                deps=[smm, msk_free], inc=DV)
                            ppr_free = am
                            ex = P.do("scalar", lambda e: e.activation(eB[:], sMSK[:], AF.Exp),
                                      deps=[am, eB_free], inc=AC)
                            msk_free = ex
                        else:
                            ex = P.do("scalar", lambda e: e.activation(eB[:], pproj[:, 0:128], AF.Exp),
                                      deps=[smm, eB_free], inc=AC)
                            ppr_free = ex
                        omm = P.do("tensor", lambda e, kt=kt, h=h, first=first, last=last: e.matmul(
                            pol[0:65, :], vtm_v[:, kt, h * 65:h * 65 + 65], eB[:],
                            start=first, stop=last), deps=[ex] + ([pol_free] if first else []), inc=TE)
                        eB_free = omm
                    evL = P.do("vector", lambda e: e.tensor_copy(sOL[:], pol[0:65, :]),
                               deps=[omm, sol_free], inc=DV)
                    pol_free = evL
                    rc = P.do("vector", lambda e: e.reciprocal(sNL[:], sOL[64:65, :]), deps=[evL], inc=DV)
                    bcm = P.do("tensor", lambda e: e.matmul(pm[0:64, 0:128], ones_row[0:1, 0:64], sNL[0:1, :],
                                                            start=True, stop=True),
                               deps=[rc, pm_free, t_ones], inc=TE)
                    cc2 = P.do("vector", lambda e, hb=hb, hc=hc, r=r: e.tensor_mul(
                        ofb_v[hb:hb + 64, hc, r * 128:(r + 1) * 128], sOL[0:64, :], pm[0:64, 0:128]),
                        deps=[bcm], inc=DV)
                    pm_free = cc2
                    sol_free = cc2
                    ob_done = cc2
            for tl in range(4):
                t = qb * 4 + tl
                par = tl % 2
                lx = P.do("sync", lambda e, t=t, par=par: e.dma_start(
                    out=xtl[par][:], in_=xsc[1 + t * 128:1 + (t + 1) * 128, :]),
                    deps=[xl_free[par], xst[t]], inc=LD, k=16)
                ev2 = None
                for hh in range(2):
                    mm = None
                    for ci in range(4):
                        mm = P.do("tensor", lambda e, ci=ci, hh=hh, tl=tl: e.matmul(
                            pproj[:, :], ofb_v[:, ci, tl * 128:(tl + 1) * 128],
                            wos[:, ci * D + hh * 512:ci * D + (hh + 1) * 512],
                            start=(ci == 0), stop=(ci == 3)),
                            deps=([ob_done, t_wo, ppr_free,
                                   ("DV", P._sems["DV"][1]), ("AC", P._sems["AC"][1])]
                                  if ci == 0 else []), inc=TE)
                    ev2 = P.do("vector", lambda e, hh=hh, par=par: e.tensor_add(
                        xtl[par][:, hh * 512:(hh + 1) * 512], xtl[par][:, hh * 512:(hh + 1) * 512],
                        pproj[:, :]), deps=[mm, lx], inc=DV)
                    ppr_free = ev2
                xst[t] = P.do("gpsimd", lambda e, t=t, par=par: e.dma_start(
                    out=xsc[1 + t * 128:1 + (t + 1) * 128, :], in_=xtl[par][:]),
                    deps=[ev2], inc=ST, k=16)
                xl_free[par] = xst[t]

        if debug:
            P.do("gpsimd", lambda e, l=l: e.dma_start(out=dbg[1 + 2 * l], in_=xsc[1:1 + 3072, :]),
                 deps=[xst[23]], inc=ST, k=16)
        rb = emit_row_bcasts(l)
        t_b1a = P.do("sync", lambda e, l=l: e.dma_start(out=b1as[:], in_=b1a_vv[l]),
                     deps=[wf.get("b1a")], inc=LD, k=16)
        t_b1g = P.do("sync", lambda e, l=l: e.dma_start(out=b1gs[:], in_=b1g_vv[l]),
                     deps=[wf.get("b1g")], inc=LD, k=16)
        t_w2 = P.do("gpsimd", lambda e, l=l: e.dma_start(
            out=w2s_v, in_=w2_vv[l]), deps=[wf.get("w2"), ("TE", P._sems["TE"][1])], inc=LD, k=16)
        hact_done = None
        sa_free = None
        sg_free = None
        pa_free = None
        pg_free = None
        for fp in range(6):
            ydep = None
            for tl in range(4):
                t = fp * 4 + tl
                par = tl % 2
                l1t = P.do("sync", lambda e, t=t, par=par: e.dma_start(
                    out=xtl[par][:, 0:512], in_=xsc[1 + t * 128:1 + (t + 1) * 128, 0:512]),
                    deps=[xl_free[par], xst[t]], inc=LD, k=16)
                l2t = P.do("sync", lambda e, t=t, par=par: e.dma_start(
                    out=xtl[par][:, 512:1024], in_=xsc[t * 128:(t + 1) * 128, 512:1024]),
                    deps=[xst[t - 1] if t > 0 else t_z0, xst[t]], inc=LD, k=16)
                rs = emit_ln_stats(xtl[par][:], [l1t, l2t])
                ty = P.do("vector", lambda e, par=par: e.tensor_scalar(
                    sY[:], xtl[par][:], st1[:, 0:1], st1[:, 4:5], ALU.subtract, ALU.mult),
                    deps=[rs, sy_free], inc=DV)
                xl_free[par] = ty
                a = P.do("vector", lambda e: e.tensor_mul(sY[:], sY[:], lngb[:]), deps=[ty, rb], inc=DV)
                a = P.do("vector", lambda e: e.tensor_add(sY[:], sY[:], lnbb[:]), deps=[a], inc=DV)
                tc = P.do("vector", lambda e: e.tensor_copy(sBF[:], sY[:]), deps=[a, sbf_free], inc=DV)
                sy_free = tc
                tts = None
                for c in range(8):
                    tts = P.do("tensor", lambda e, c=c: e.transpose(
                        pt[:, (c % 4) * 128:(c % 4) * 128 + 128], sBF[:, c * 128:(c + 1) * 128], idb[:]),
                        deps=[tc] + ([pt_free] if c % 4 == 0 else []), inc=TE)
                    if c == 3 or c == 7:
                        ydep = P.do("vector", lambda e, c=c, tl=tl: e.tensor_copy(
                            hg_v[:, c - 3:c + 1, tl * 128:(tl + 1) * 128],
                            pt.rearrange("p (cc k) -> p cc k", cc=4)[:, :, :]), deps=[tts], inc=DV)
                        pt_free = ydep
                sbf_free = tts
            for oc in range(22):
                b = oc % 2
                wl1 = P.do("gpsimd", lambda e, l=l, oc=oc, b=b: e.dma_start(
                    out=w1as[b][:].rearrange("p (c k) -> p c k", c=8),
                    in_=w1a_vv[l][:, :, oc * 128:(oc + 1) * 128]), deps=[wf.get(("w1a", b))], inc=LD, k=16)
                wl2 = P.do("gpsimd", lambda e, l=l, oc=oc, b=b: e.dma_start(
                    out=w1gs[b][:].rearrange("p (c k) -> p c k", c=8),
                    in_=w1g_vv[l][:, :, oc * 128:(oc + 1) * 128]), deps=[wf.get(("w1g", b))], inc=LD, k=16)
                mm1 = None
                for ci in range(8):
                    mm1 = P.do("tensor", lambda e, ci=ci, b=b: e.matmul(
                        pa[:, :], w1as[b][:, ci * 128:(ci + 1) * 128], hg_v[:, ci, :],
                        start=(ci == 0), stop=(ci == 7)),
                        deps=([wl1, ydep, pa_free, ("DV", P._sems["DV"][1]),
                               ("AC", P._sems["AC"][1])] if ci == 0 else []), inc=TE)
                wf[("w1a", b)] = mm1
                mm2 = None
                for ci in range(8):
                    mm2 = P.do("tensor", lambda e, ci=ci, b=b: e.matmul(
                        pg[:, :], w1gs[b][:, ci * 128:(ci + 1) * 128], hg_v[:, ci, :],
                        start=(ci == 0), stop=(ci == 7)),
                        deps=([wl2, pg_free] if ci == 0 else []), inc=TE)
                wf[("w1g", b)] = mm2
                aa = P.do("vector", lambda e, oc=oc: e.tensor_scalar(
                    sA[:], pa[:, :], b1as[:, oc:oc + 1], None, ALU.add),
                    deps=[mm1, t_b1a, sa_free], inc=DV)
                pa_free = aa
                gg = P.do("scalar", lambda e, oc=oc: e.activation(
                    sG[:], pg[:, :], AF.Gelu, bias=b1gs[:, oc:oc + 1]),
                    deps=[mm2, t_b1g, sg_free], inc=AC)
                pg_free = gg
                hact_done = P.do("vector", lambda e, oc=oc: e.tensor_tensor(
                    hact_v[:, oc, :], sA[:], sG[:], ALU.mult), deps=[aa, gg, hact_done], inc=DV)
                sa_free = hact_done
                sg_free = hact_done
            if debug and l == 0 and fp == 0:
                P.do("gpsimd", lambda e: e.dma_start(out=dbg[5][0:128, :], in_=hact[:, 0:1024]),
                     deps=[hact_done], inc=ST, k=16)
                P.do("gpsimd", lambda e: e.dma_start(out=dbg[5][128:256, 0:512], in_=sA[:]),
                     deps=[hact_done], inc=ST, k=16)
                P.do("gpsimd", lambda e: e.dma_start(out=dbg[5][256:384, 0:512], in_=sG[:]),
                     deps=[hact_done], inc=ST, k=16)
                P.do("gpsimd", lambda e: e.dma_start(out=dbg[5][384:512, :], in_=hg[:, 0:1024]),
                     deps=[hact_done], inc=ST, k=16)
            for tl in range(4):
                t = fp * 4 + tl
                par = tl % 2
                lx = P.do("sync", lambda e, t=t, par=par: e.dma_start(
                    out=xtl[par][:], in_=xsc[1 + t * 128:1 + (t + 1) * 128, :]),
                    deps=[xl_free[par], xst[t]], inc=LD, k=16)
                ev2 = None
                for hh in range(2):
                    mm = None
                    for oc in range(22):
                        mm = P.do("tensor", lambda e, oc=oc, hh=hh, tl=tl: e.matmul(
                            [pa, pg][hh][:, :], hact_v[:, oc, tl * 128:(tl + 1) * 128],
                            w2s_v[:, oc, hh * 512:(hh + 1) * 512],
                            start=(oc == 0), stop=(oc == 21)),
                            deps=([hact_done, t_w2, ("DV", P._sems["DV"][1]), ("AC", P._sems["AC"][1])]
                                  + [[pa_free, pg_free][hh]] if oc == 0 else []), inc=TE)
                    a1 = P.do("vector", lambda e, hh=hh, par=par: e.tensor_add(
                        xtl[par][:, hh * 512:(hh + 1) * 512], xtl[par][:, hh * 512:(hh + 1) * 512],
                        b2b[:, hh * 512:(hh + 1) * 512]),
                        deps=[lx, rb], inc=DV)
                    ev2 = P.do("vector", lambda e, hh=hh, par=par: e.tensor_add(
                        xtl[par][:, hh * 512:(hh + 1) * 512], xtl[par][:, hh * 512:(hh + 1) * 512],
                        [pa, pg][hh][:, :]), deps=[a1, mm], inc=DV)
                    if hh == 0:
                        pa_free = ev2
                    else:
                        pg_free = ev2
                xst[t] = P.do("gpsimd", lambda e, t=t, par=par: e.dma_start(
                    out=xsc[1 + t * 128:1 + (t + 1) * 128, :], in_=xtl[par][:]),
                    deps=[ev2], inc=ST, k=16)
                xl_free[par] = xst[t]
        if debug:
            P.do("gpsimd", lambda e, l=l: e.dma_start(out=dbg[2 + 2 * l], in_=xsc[1:1 + 3072, :]),
                 deps=[xst[23]], inc=ST, k=16)
        wf["wqkv"] = kq_done
        wf["wo"] = ppr_free
        wf["w2"] = mm
        wf["arena"] = ev2
        wf["b1a"] = aa
        wf["b1g"] = gg

    sy_used = None
    for t in range(NT):
        par = t % 2
        lx = P.do("sync", lambda e, t=t, par=par: e.dma_start(out=xtl[par][:], in_=xv[:, t, :]),
                  deps=[xl_free[par]], inc=LD, k=16)
        et = 8 + t
        lx2 = P.do("sync", lambda e, et=et: e.dma_start(
            out=sY[:], in_=xsc[1 + et * 128:1 + (et + 1) * 128, :]),
            deps=[sy_used, sy_free, xst[et]], inc=LD, k=16)
        tf = P.do("vector", lambda e, par=par: e.tensor_add(xtl[par][:], xtl[par][:], sY[:]),
                  deps=[lx, lx2], inc=DV)
        sy_used = tf
        sy_free = tf
        xl_free[par] = P.do("gpsimd", lambda e, t=t, par=par: e.dma_start(
            out=xov[:, t, :], in_=xtl[par][:]), deps=[tf], inc=ST, k=16)

    P.wait("gpsimd", *P.dma_done("ST"))
    return P.finalize()


def prep_l3_ffn(d, xflat):
    DIMH = 64
    WINDOW = 512
    HID = 2816
    w1 = d["L3_ffn_w1"]
    w2 = d["L3_ffn_w2"]
    b1 = d["L3_ffn_b1"]
    w1a = np.concatenate([np.pad(w1[l][:, :2730], ((0, 0), (0, 86))) for l in range(2)], 0)
    w1g = np.concatenate([np.pad(w1[l][:, 2730:], ((0, 0), (0, 86))) for l in range(2)], 0)
    w2p = np.concatenate([np.pad(w2[l], ((0, 86), (0, 0))) for l in range(2)], 0)
    b1a = np.concatenate([np.ascontiguousarray(
        np.pad(b1[l][:2730], (0, 86)).reshape(22, 128).T) for l in range(2)], 0)
    b1g = np.concatenate([np.ascontiguousarray(
        np.pad(b1[l][2730:], (0, 86)).reshape(22, 128).T) for l in range(2)], 0)
    qks = np.stack([d["L3_ffn_q_scale"][0], d["L3_ffn_k_scale"][0],
                    d["L3_ffn_q_scale"][1], d["L3_ffn_k_scale"][1]], 1)
    qksc = np.ascontiguousarray(np.tile(qks, (2, 1)).astype(np.float32))
    lnrows = np.stack([d["L3_ffn_ln_g"][0], d["L3_ffn_ln_b"][0], d["L3_ffn_b2"][0],
                       d["L3_ffn_ln_g"][1], d["L3_ffn_ln_b"][1], d["L3_ffn_b2"][1]])
    jl = np.arange(128)[:, None]
    il = np.arange(128)[None, :]
    MS = np.where(il >= jl, 0.0, -1e30).astype(np.float32)
    MP = np.where(il <= jl, 0.0, -1e30).astype(np.float32)
    MINV = np.full((128, 128), -1e30, np.float32)
    MZ = np.zeros((128, 128), np.float32)
    M = np.zeros((128, 128), dtype=np.float32)
    for di in range(128):
        if di % 64 < 32:
            M[di, di + 32] = -1.0
        else:
            M[di, di - 32] = 1.0
    ones2 = np.zeros((128, 2), np.float32)
    ones2[0:64, 0] = 1.0
    ones2[64:128, 1] = 1.0
    half = np.arange(0, DIMH, 2, dtype=np.float32)
    inv_freq = (1.0 / 10000.0 ** (half / DIMH)).astype(np.float32)
    base = ((half + 0.4 * DIMH) / (1.4 * DIMH)).astype(np.float32)
    com = {
        "wqkv": np.ascontiguousarray(d["L3_ffn_wqkv"].reshape(2 * 1024, 1536)),
        "wo": np.ascontiguousarray(d["L3_ffn_wo"].reshape(2 * 512, 1024)),
        "w1a": np.ascontiguousarray(w1a), "w1g": np.ascontiguousarray(w1g),
        "w2m": np.ascontiguousarray(w2p),
        "b1ac": b1a, "b1gc": b1g, "qksc": qksc,
        "lnrows": np.ascontiguousarray(lnrows),
        "msame": MS, "mprev": MP,
        "rt": np.ascontiguousarray(M.T),
        "ones2m": ones2, "e2m": np.ascontiguousarray(ones2.T),
        "ident": np.eye(128, dtype=np.float32),
    }
    maps = []
    EXT = 3072
    for c in range(NCORE):
        b, so = c // 4, (c % 4) * 2048
        xb = xflat[b * 8192:(b + 1) * 8192]
        if so == 0:
            ctx = np.zeros((1024, D), np.float32)
        else:
            ctx = xb[so - 1024:so]
        m = dict(com)
        m["x"] = np.ascontiguousarray(xflat[c * TOK:(c + 1) * TOK])
        m["xe"] = np.ascontiguousarray(np.concatenate([ctx, xb[so:so + 2048]], 0))
        pos = (so - 1024 + np.arange(EXT)).astype(np.float32)
        fr = pos[None, :] * inv_freq[:, None]
        power = (pos - 8192 // 2) / float(WINDOW // 2)
        scale = base[:, None] ** power[None, :]
        cos = np.cos(fr); sin = np.sin(fr)
        cqt = np.tile(cos * scale * 8.0, (4, 1)).astype(np.float32)
        sqt = np.tile(sin * scale * 8.0, (4, 1)).astype(np.float32)
        ckt = np.tile(cos / scale, (4, 1)).astype(np.float32)
        skt = np.tile(sin / scale, (4, 1)).astype(np.float32)
        m["cq"], m["sq"], m["ckt"], m["skt"] = cqt, sqt, ckt, skt
        m["mpfp"] = MINV if so == 0 else MP
        m["mpff"] = MINV if so == 0 else MZ
        maps.append(m)
    return maps


_L3_FFN_NC = None


def launch_l3_ffn(xflat, d, debug=False):
    global _L3_FFN_NC
    if _L3_FFN_NC is None or debug:
        _L3_FFN_NC = build_l3_ffn(debug)
    in_maps = prep_l3_ffn(d, xflat)
    res = run_bass_kernel_spmd(_L3_FFN_NC, in_maps, core_ids=list(range(NCORE)))
    out = np.concatenate([res.results[c]["xo"] for c in range(NCORE)], axis=0)
    if debug:
        return out, res.results[1].get("dbg")
    return out
